# version 09
# baseline (speedup 1.0000x reference)
import contextlib
import numpy as np
import concourse.bass as bass
import concourse.mybir as mybir
from concourse.bass_utils import run_bass_kernel_spmd

F32 = mybir.dt.float32
BF16 = mybir.dt.bfloat16
I32 = mybir.dt.int32
AF = mybir.ActivationFunctionType
ALU = mybir.AluOpType
AX = mybir.AxisListType

NCORE = 8
D = 2048
L = 8192
HID = 5632
TPC = L // NCORE
CPC = D // NCORE
NTC = TPC // 128
NDC = D // 128
NHC = HID // 128
NM = L // 128
ND = 17
KP = 2304
PADL = 8
EPS = 1e-5

PV_BQ, PV_BK, PV_BV, PV_CQB, PV_CKB, PV_DSK = 0, 2, 4, 6, 8, 10
PV_CQW, PV_CKW, PV_SG, PV_SU = 12, 18, 24, 68
NPV = 112


class Ch:

    def __init__(self, blk, sem, qsems):
        self.blk = blk
        self.s = sem
        self.q = qsems
        self.pend = []

    def step(self, eng, emit, k=1):
        s = self.s
        pend = list(self.pend)

        def _(e):
            e.wait_ge(s[0], s[1] - k)
            for sm, v in pend:
                e.wait_ge(sm[0], v)
            emit(e).then_inc(s[0], k)
        s[1] += k
        self.pend = []
        getattr(self.blk, eng)(_)

    def dma(self, eng, out, in_):
        self.dmas(eng, [(out, in_)])

    def dmas(self, eng, pairs):
        s = self.s
        qs = self.q[eng]
        pend = list(self.pend)

        def _(e):
            e.wait_ge(s[0], s[1])
            for sm, v in pend:
                e.wait_ge(sm[0], v)
            for out, in_ in pairs:
                e.dma_start(out=out, in_=in_).then_inc(qs[0], 16)
        qs[1] += 16 * len(pairs)
        self.pend = [(qs, qs[1])]
        getattr(self.blk, eng)(_)

    def join(self, eng, sem, val):
        s = self.s
        pend = list(self.pend)

        def _(e):
            e.wait_ge(sem[0], val)
            e.wait_ge(s[0], s[1] - 1)
            for sm, v in pend:
                e.wait_ge(sm[0], v)
            (e.engine_nop() if hasattr(e, "engine_nop") else e.nop()) \
                .then_inc(s[0], 1)
        s[1] += 1
        self.pend = []
        getattr(self.blk, eng)(_)

    def snap(self):
        if self.pend:
            self.step("vector", lambda v: v.engine_nop())
        return self.s[1]


def build_nc(tap=()):
    nc = bass.Bass("TRN2", target_bir_lowering=False, debug=False)

    u_r = nc.dram_tensor("u_r", [TPC, D], F32, kind="ExternalInput").ap()
    wqT = nc.dram_tensor("wqT", [D, CPC], F32, kind="ExternalInput").ap()
    wkT = nc.dram_tensor("wkT", [D, CPC], F32, kind="ExternalInput").ap()
    wvT = nc.dram_tensor("wvT", [D, CPC], F32, kind="ExternalInput").ap()
    pvec = nc.dram_tensor("pvec", [128, NPV], F32, kind="ExternalInput").ap()
    ssm = nc.dram_tensor("ssm", [CPC, 2048], F32, kind="ExternalInput").ap()
    woT = nc.dram_tensor("woT", [D, D], F32, kind="ExternalInput").ap()
    bo_d = nc.dram_tensor("bo_d", [1, D], F32, kind="ExternalInput").ap()
    sdn = nc.dram_tensor("sdn", [1, D], F32, kind="ExternalInput").ap()
    wgT = nc.dram_tensor("wgT", [D, HID], I32, kind="ExternalInput").ap()
    wuT = nc.dram_tensor("wuT", [D, HID], I32, kind="ExternalInput").ap()
    wdT = nc.dram_tensor("wdT", [HID, D], I32, kind="ExternalInput").ap()
    out_r = nc.dram_tensor("out_r", [TPC, D], F32, kind="ExternalOutput").ap()

    ln1cin = nc.dram_tensor("ln1cin", [D, TPC], BF16)
    ln1all = nc.dram_tensor("ln1all", [NCORE * D, TPC], BF16,
                            addr_space="Shared")
    y2cin = nc.dram_tensor("y2cin", [CPC, L], BF16)
    y2h = [nc.dram_tensor(f"y2h{i}", [NCORE * 128, L], BF16,
                          addr_space="Shared") for i in range(2)]
    kpbuf = nc.dram_tensor("kpbuf", [CPC, KP], BF16)
    x1sp = nc.dram_tensor("x1sp", [TPC, D], F32)
    wdbf = nc.dram_tensor("wdbf", [HID, D], BF16)

    taps = {}
    for name, shape, dt in tap:
        taps[name] = nc.dram_tensor(name, shape, dt, kind="ExternalOutput").ap()

    es = contextlib.ExitStack()
    es_ln = contextlib.ExitStack()
    es_mix = contextlib.ExitStack()
    es_qkv = contextlib.ExitStack()
    es_ln2t = contextlib.ExitStack()
    es_wo = contextlib.ExitStack()
    es_p5 = contextlib.ExitStack()

    def sbE(stack, name, shape, dtype, side="left"):
        return stack.enter_context(nc.sbuf_tensor(name, shape, dtype, side=side))

    pv = sbE(es, "pv", [128, NPV], F32)
    ident = sbE(es, "ident", [128, 128], BF16)
    ones1 = sbE(es, "ones1", [1, 128], BF16)
    cst = sbE(es, "cst", [128, 2], F32)
    x0t = sbE(es, "x0t", [128, D], F32)
    scr = sbE(es, "scr", [128, D], BF16)
    tmpf = sbE(es, "tmpf", [128, D], F32)
    sdrep = sbE(es, "sdrep", [128, D], F32)
    stB = sbE(es, "stB", [128, 24 * NTC], F32)
    stA = sbE(es, "stA", [128, 2 * NTC], F32)
    stM = sbE(es, "stM", [128, NTC], F32)
    stR = sbE(es, "stR", [128, NTC], F32)

    qk_shape = [128, PADL + L]
    q0 = sbE(es_mix, "q0", qk_shape, BF16, side="right")
    q1 = sbE(es_mix, "q1", qk_shape, BF16, side="right")
    k0 = sbE(es_mix, "k0", qk_shape, BF16, side="right")
    k1 = sbE(es_mix, "k1", qk_shape, BF16, side="right")
    v0 = sbE(es_mix, "v0", [128, L], BF16, side="right")
    v1 = sbE(es_mix, "v1", [128, L], BF16, side="right")
    QU, KU, VU = (q0, q1), (k0, k1), (v0, v1)

    x0B = sbE(es_ln, "x0B", [128, NTC * D], BF16, side="right")
    lnTB = sbE(es_ln, "lnTB", [128, NDC * TPC], BF16, side="right")

    pA = es.enter_context(nc.psum_tensor("pA", [128, 512], F32))
    pB = es.enter_context(nc.psum_tensor("pB", [128, 512], F32))
    pC = es.enter_context(nc.psum_tensor("pC", [128, 512], F32))
    pD = es.enter_context(nc.psum_tensor("pD", [128, 512], F32))
    pE = es.enter_context(nc.psum_tensor("pE", [128, 512], F32))
    pF = es.enter_context(nc.psum_tensor("pF", [128, 512], F32))
    pT1 = es_p5.enter_context(nc.psum_tensor("pT1", [128, 1024], BF16,
                                             side="right"))
    pT2 = es_p5.enter_context(nc.psum_tensor("pT2", [128, 1024], BF16,
                                             side="right"))
    PQ = [pA, pB, pC, pD]
    PT = [pT1, pT2]

    def sem(name):
        return [es.enter_context(nc.semaphore(name)), 0]

    s_ch = sem("s_ch")
    d_rhs = [sem("d_rhs0"), sem("d_rhs1")]
    m_qkv, m_qev = sem("m_qkv"), sem("m_qev")
    m_tr, m_tev = sem("m_tr"), sem("m_tev")
    d_ga = [sem("d_ga0"), sem("d_ga1"), sem("d_ga2"), sem("d_ga3")]
    d_gb = [sem("d_gb0"), sem("d_gb1"), sem("d_gb2"), sem("d_gb3")]
    m_cv, m_cev = sem("m_cv"), sem("m_cev")
    s_c1, s_c2 = sem("s_c1"), sem("s_c2")
    s_c1b, s_c2b, s_c2c, s_c2d = sem("s_c1b"), sem("s_c2b"), sem("s_c2c"), sem("s_c2d")
    d_wc = [sem("d_wc0"), sem("d_wc1"), sem("d_wc2"), sem("d_wc3")]
    w_wc = [sem("w_wc0"), sem("w_wc1"), sem("w_wc2"), sem("w_wc3")]
    m_wc = sem("m_wc")
    m_wo, m_aoev = sem("m_wo"), sem("m_aoev")
    d_fg = [sem("d_fg0"), sem("d_fg1")]
    d_fu = [sem("d_fu0"), sem("d_fu1")]
    m_fpe, m_fsi, m_fhd = sem("m_fpe"), sem("m_fsi"), sem("m_fhd")
    d_wd = [sem("d_wd0"), sem("d_wd1"), sem("d_wd2"), sem("d_wd3")]
    m_dn, m_dwev = sem("m_dn"), sem("m_dwev")
    d_p5, w_x1, m_5a = sem("d_p5"), sem("w_x1"), sem("m_5a")
    d_p5b, m_5n = sem("d_p5b"), sem("m_5n")
    d_p8, m_8, w_out = sem("d_p8"), sem("m_8"), sem("w_out")
    m_8s = sem("m_8s")

    def AP(t, off, dims):
        return bass.AP(t, off, dims)

    nc.const_aps.register(F32, float(EPS), cst[:, 0:1])
    nc.const_aps.register(F32, -0.5, cst[:, 1:2])

    qsems = {"sync": sem("q_sy"), "gpsimd": sem("q_gp"),
             "scalar": sem("q_sc"), "vector": sem("q_dv")}
    blk = es.enter_context(nc.Block())
    ch = Ch(blk, s_ch, qsems)

    def bn_batch(src_sb, src_pitch, offs):
        for j, o in enumerate(offs):
            for q4 in range(4):
                def st(v, j=j, o=o, q4=q4):
                    return v.bn_stats(
                        stB[:, 24 * j + 6 * q4:24 * j + 6 * (q4 + 1)],
                        AP(src_sb, o + q4 * 512, [[src_pitch, 128], [1, 512]]))
                ch.step("vector", st)
        for j in range(len(offs)):
            def ag(v, j=j):
                return v.bn_aggr(stA[:, 2 * j:2 * j + 2],
                                 stB[:, 24 * j:24 * (j + 1)])
            ch.step("vector", ag)

    def stats_finish(n):
        mview = AP(stA, 0, [[2 * NTC, 128], [2, n]])
        vview = AP(stA, 1, [[2 * NTC, 128], [2, n]])

        def neg(v):
            return v.tensor_scalar(stM[:, 0:n], mview, -1.0, None, ALU.mult)
        ch.step("vector", neg)

        def ln_(sc):
            return sc.activation(stR[:, 0:n], vview, AF.Ln, bias=float(EPS))
        ch.step("scalar", ln_)

        def ex_(sc):
            return sc.activation(stR[:, 0:n], stR[:, 0:n], AF.Exp, scale=-0.5)
        ch.step("scalar", ex_)

    def p0a_ident(gp):
        return gp.memset(ident[:, :], 0.0)
    ch.step("gpsimd", p0a_ident, 1)

    def p0a_sets(gp):
        gp.affine_select(out=ident[:, :], in_=ident[:, :],
                         compare_op=ALU.not_equal, fill=1.0, base=0,
                         pattern=[[-1, 128]], channel_multiplier=1)
        gp.memset(ones1[:, :], 1.0)
        gp.memset(cst[:, 0:1], float(EPS))
        gp.memset(cst[:, 1:2], -0.5)
        ins = None
        for u in (q0, q1, k0, k1):
            ins = gp.memset(u[:, 0:PADL], 0.0)
        return ins
    ch.step("gpsimd", p0a_sets, 1)

    x0B_dst = AP(x0B, 0, [[NTC * D, 128], [D, NTC], [1, D]])
    x0B_src = AP(u_r.tensor, 0, [[D, 128], [128 * D, NTC], [1, D]])
    ch.dmas("gpsimd", [(pv[:, :], pvec), (x0B_dst, x0B_src)])

    bn_batch(x0B, NTC * D, [tc * D for tc in range(NTC)])
    stats_finish(NTC)
    for tc in range(NTC):
        def nrm(v, tc=tc):
            sl = AP(x0B, tc * D, [[NTC * D, 128], [1, D]])
            return v.tensor_scalar(sl, sl, stM[:, tc:tc + 1],
                                   stR[:, tc:tc + 1], ALU.add, ALU.mult)
        ch.step("vector", nrm)

    trn = [0]

    def pipe_transpose(src_fn, dst_fn, ngroups, ch_gate, gate_sem=None,
                       gate_val=0):
        t0 = trn[0]

        def pe(t, t0=t0, ch_gate=ch_gate):
            t.wait_ge(s_ch[0], ch_gate)
            if gate_sem is not None:
                t.wait_ge(gate_sem[0], gate_val)
            for g in range(ngroups):
                n = t0 + g
                pt = PT[n % 2]
                if n >= 2:
                    t.wait_ge(m_tev[0], n - 1)
                ins = None
                for j in range(4):
                    ins = t.transpose(pt[:, bass.ts(j, 128)], src_fn(g, j),
                                      ident[:, :])
                ins.then_inc(m_tr[0], 1)
        blk.tensor(pe)
        m_tr[1] += ngroups

        def act(sc, t0=t0):
            for g in range(ngroups):
                n = t0 + g
                pt = PT[n % 2]
                sc.wait_ge(m_tr[0], n + 1)
                sc.copy(dst_fn(g), AP(pt, 0, [[1024, 128], [128, 4], [1, 128]])) \
                    .then_inc(m_tev[0], 1)
        blk.scalar(act)
        m_tev[1] += ngroups
        trn[0] += ngroups

    gate0 = ch.snap()
    tev0 = m_tev[1]
    pipe_transpose(
        lambda g, j: AP(x0B, (g // 4) * D + ((g % 4) * 4 + j) * 128,
                        [[NTC * D, 128], [1, 128]]),
        lambda g: AP(lnTB, ((g % 4) * 4) * TPC + (g // 4) * 128,
                     [[NDC * TPC, 128], [TPC, 4], [1, 128]]),
        NTC * 4, gate0)
    ch.join("vector", m_tev, tev0 + 32)
    ch.dma("scalar",
           AP(ln1cin, 0, [[TPC, 128], [128 * TPC, NDC], [1, TPC]]),
           AP(lnTB, 0, [[NDC * TPC, 128], [TPC, NDC], [1, TPC]]))
    pend1 = list(ch.pend)

    def coll1(gp, pend1=pend1, chv=s_ch[1]):
        gp.wait_ge(s_ch[0], chv)
        for sm, v in pend1:
            gp.wait_ge(sm[0], v)
        gp.collective_compute(
            "AllGather", ALU.bypass, replica_groups=[list(range(NCORE))],
            ins=[ln1cin.ap().opt()], outs=[ln1all.ap().opt()]) \
            .then_inc(s_c1[0])
    blk.gpsimd(coll1)

    es_ln.close()
    wds4 = sbE(es, "wds4", [128, 3 * D], BF16)
    wq_sb = sbE(es_qkv, "wq_sb", [128, NDC * CPC], BF16, side="right")
    wk_sb = sbE(es_qkv, "wk_sb", [128, NDC * CPC], BF16, side="right")
    wv_sb = sbE(es_qkv, "wv_sb", [128, NDC * CPC], BF16, side="right")
    kb = sbE(es_qkv, "kb", [128, 2 * KP], BF16, side="right")
    rhsb = sbE(es_qkv, "rhsb", [128, 2 * NDC * 512], BF16, side="right")

    def p0c_kb(gp):
        return gp.memset(kb[:, :], 0.0)
    ch.step("gpsimd", p0c_kb, 1)
    loadpairs = []
    for wsb, wdr in ((wq_sb, wqT), (wk_sb, wkT), (wv_sb, wvT)):
        dst = AP(wsb, 0, [[NDC * CPC, 128], [CPC, NDC], [1, CPC]])
        src = AP(wdr.tensor, 0, [[CPC, 128], [128 * CPC, NDC], [1, CPC]])
        loadpairs.append((dst, src))
    for cc in range(2):
        dst = AP(kb, cc * KP + 127, [[2 * KP, 128], [1, 2048]])
        src = AP(ssm.tensor, cc * 128 * 2048, [[2048, 128], [1, 2048]])
        loadpairs.append((dst, src))
    ch.dmas("gpsimd", loadpairs)
    ch.dmas("scalar", [
        (AP(kpbuf, cc * 128 * KP, [[KP, 128], [1, KP]]),
         AP(kb, cc * KP, [[2 * KP, 128], [1, KP]])) for cc in range(2)])
    ch.join("gpsimd", s_c1, 1)
    if "t_ln1" in taps:
        dstt = AP(taps["t_ln1"].tensor, 0,
                  [[TPC, NCORE * D], [512, 2], [1, 512]])
        srct = AP(ln1all, 0, [[512, NCORE * D], [D * 512, 2], [1, 512]])
        srct = bass.AP(ln1all, 0, [[512, NCORE * D], [D * 512, 2], [1, 512]])
        dstt = AP(taps["t_ln1"].tensor, 0,
                  [[D * TPC, NCORE], [TPC, D], [512, 2], [1, 512]])
        srct = AP(ln1all, 0, [[2 * D * 512, NCORE], [512, D], [D * 512, 2],
                              [1, 512]])
        ch.dma("scalar", dstt, srct)

    ch_qkv = ch.snap()
    WSB = (wq_sb, wk_sb, wv_sb)
    UNITS = (QU, KU, VU)
    BIASC = (PV_BQ, PV_BK, PV_BV)

    ch_wc = ch_qkv

    def wc_gp(gp):
        gp.wait_ge(s_ch[0], ch_wc)
        for hc in range(NHC + 2):
            if hc < NHC:
                sl = d_wc[hc % 3]
                gp.wait_ge(sl[0], 16 * (hc // 3))
                if hc >= 3:
                    gp.wait_ge(w_wc[hc % 3][0], 16 * (hc // 3))
                dst = AP(wds4, (hc % 3) * D, [[3 * D, 128], [1, D]])
                src = AP(wdT.tensor, hc * 128 * D, [[D, 128], [1, D]])
                gp.dma_start(out=dst, in_=src).then_inc(sl[0], 16)
            wb = hc - 2
            if 0 <= wb < NHC:
                gp.wait_ge(m_wc[0], wb + 1)
                slw = w_wc[wb % 3]
                gp.wait_ge(slw[0], 16 * (wb // 3))
                wdst = AP(wdbf, wb * 128 * D, [[D, 128], [1, D]])
                wsrc = AP(wds4, (wb % 3) * D, [[3 * D, 128], [1, D]])
                gp.dma_start(out=wdst, in_=wsrc).then_inc(slw[0], 16)
    blk.gpsimd(wc_gp)
    for i4 in range(3):
        d_wc[i4][1] += len(range(i4, NHC, 3)) * 16
        w_wc[i4][1] += len(range(i4, NHC, 3)) * 16

    def wc_dw(v):
        v.wait_ge(s_ch[0], ch_wc)
        for hc in range(NHC):
            v.wait_ge(d_wc[hc % 3][0], 16 * (hc // 3 + 1))
            sl = AP(wds4, (hc % 3) * D, [[3 * D, 128], [1, D]])
            v.tensor_scalar_add(sl, sl, -128.0).then_inc(m_wc[0], 1)
    blk.vector(wc_dw)
    m_wc[1] += NHC

    def p1_sync(sy):
        sy.wait_ge(s_ch[0], ch_qkv)
        sy.wait_ge(s_c1[0], 1)
        for i in range(16):
            sl = d_rhs[i % 2]
            sy.wait_ge(sl[0], 16 * (i // 2))
            if i >= 2:
                sy.wait_ge(m_qkv[0], (i - 1) * 6)
            r, h = i // 2, i % 2
            dst = AP(rhsb, (i % 2) * NDC * 512,
                     [[2 * NDC * 512, 128], [512, NDC], [1, 512]])
            src = AP(ln1all, r * D * TPC + h * 512,
                     [[TPC, 128], [128 * TPC, NDC], [1, 512]])
            sy.dma_start(out=dst, in_=src).then_inc(sl[0], 16)
    blk.sync(p1_sync)
    d_rhs[0][1] += 16 * 8
    d_rhs[1][1] += 16 * 8

    def p1_pe(t):
        for i in range(16):
            t.wait_ge(d_rhs[i % 2][0], 16 * (i // 2 + 1))
            for jj in range(6):
                n = 6 * i + jj
                proj, cc = jj // 2, jj % 2
                bank = PQ[n % 4]
                if n >= 4:
                    t.wait_ge(m_qev[0], n - 3)
                ins = None
                for dc in range(NDC):
                    lhsT = AP(WSB[proj], dc * CPC + cc * 128,
                              [[NDC * CPC, 128], [1, 128]])
                    rhs = AP(rhsb, (i % 2) * NDC * 512 + dc * 512,
                             [[2 * NDC * 512, 128], [1, 512]])
                    ins = t.matmul(bank[:, :], lhsT, rhs,
                                   start=(dc == 0), stop=(dc == NDC - 1))
                ins.then_inc(m_qkv[0], 1)
    blk.tensor(p1_pe)
    m_qkv[1] += 96

    def p1_act(sc):
        for n in range(96):
            sc.wait_ge(m_qkv[0], n + 1)
            i, jj = n // 6, n % 6
            proj, cc = jj // 2, jj % 2
            unit = UNITS[proj][cc]
            pitch = (PADL + L) if proj < 2 else L
            off = (PADL if proj < 2 else 0) + i * 512
            dst = AP(unit, off, [[pitch, 128], [1, 512]])
            sc.add(dst, PQ[n % 4][:, :],
                   pv[:, BIASC[proj] + cc:BIASC[proj] + cc + 1]) \
                .then_inc(m_qev[0], 1)
    blk.scalar(p1_act)
    m_qev[1] += 96

    ch.join("vector", m_qev, 96)

    PQK = PADL + L

    def emit_sconv(src_unit, dst_sb, dst_pitch, wcol, bcol):
        def c1(v):
            return v.tensor_scalar(
                AP(dst_sb, 0, [[dst_pitch, 128], [1, L]]),
                AP(src_unit, PADL, [[PQK, 128], [1, L]]),
                pv[:, wcol + 4:wcol + 5], pv[:, bcol:bcol + 1],
                ALU.mult, ALU.add)
        ch.step("vector", c1)

        def c2(v):
            return v.scalar_tensor_tensor(
                AP(dst_sb, 0, [[dst_pitch, 128], [1, L]]),
                AP(src_unit, PADL - 1, [[PQK, 128], [1, L]]),
                pv[:, wcol + 2:wcol + 3],
                AP(dst_sb, 0, [[dst_pitch, 128], [1, L]]), ALU.mult, ALU.add)
        ch.step("vector", c2)

        def c3(v):
            return v.scalar_tensor_tensor(
                AP(dst_sb, 0, [[dst_pitch, 128], [1, L]]),
                AP(src_unit, PADL - 2, [[PQK, 128], [1, L]]),
                pv[:, wcol + 0:wcol + 1],
                AP(dst_sb, 0, [[dst_pitch, 128], [1, L]]), ALU.mult, ALU.add)
        ch.step("vector", c3)

    for cc in range(2):
        emit_sconv(KU[cc], rhsb, 2 * NDC * 512, PV_CKW + cc, PV_CKB + cc)

        def mkv(v, cc=cc):
            return v.tensor_mul(VU[cc][:, :],
                                AP(rhsb, 0, [[2 * NDC * 512, 128], [1, L]]),
                                VU[cc][:, :])
        ch.step("vector", mkv)
    for cc in range(2):
        emit_sconv(QU[cc], KU[cc], PQK, PV_CQW + cc, PV_CQB + cc)

    if "t_kv" in taps:
        for cc in range(2):
            ch.dma("scalar", AP(taps["t_kv"].tensor, cc * 128 * L,
                                [[L, 128], [1, L]]), VU[cc][:, :])
    if "t_qt" in taps:
        for cc in range(2):
            ch.dma("scalar", AP(taps["t_qt"].tensor, cc * 128 * L,
                                [[L, 128], [1, L]]),
                   AP(KU[cc], 0, [[PQK, 128], [1, L]]))

    es_qkv.close()
    kvr = sbE(es_mix, "kvr", [128, L], BF16, side="right")
    kvr2 = sbE(es_mix, "kvr2", [128, L], BF16, side="right")
    xA = sbE(es_mix, "xA", [128, L], BF16, side="right")
    gbuf = sbE(es_mix, "gbuf", [128, 4 * ND * 128], BF16, side="right")

    KVR = (kvr, kvr2)
    for cc in range(2):
        REVU = QU[cc]

        def rev(v, cc=cc, REVU=REVU):
            dst = AP(REVU, 0, [[PQK, 128], [128, NM], [1, 128]])
            src = AP(VU[cc], 127, [[L, 128], [128, NM], [-1, 128]])
            return v.tensor_copy(dst, src)
        ch.step("vector", rev)

        ch_here = ch.snap()
        pipe_transpose(
            lambda g, j, REVU=REVU: AP(REVU, (g * 4 + j) * 128,
                                       [[PQK, 128], [1, 128]]),
            lambda g, cc=cc: KVR[cc][:, bass.ts(g, 512)],
            16, ch_here)
    ch.join("vector", m_tev, m_tev[1])

    ch_cv = ch.snap()
    T_y1 = [trn[0] + 16, trn[0] + 32]

    def p3_sy(sy):
        sy.wait_ge(s_ch[0], ch_cv)
        for c in range(256):
            sl = d_ga[c % 4]
            sy.wait_ge(sl[0], 16 * (c // 4))
            if c >= 4:
                sy.wait_ge(m_cv[0], c - 3)
            dst = AP(gbuf, (c % 4) * ND * 128,
                     [[4 * ND * 128, 128], [1, ND * 128]])
            src = AP(kpbuf, c * KP, [[1, 128], [1, ND * 128]])
            sy.dma_start(out=dst, in_=src).then_inc(sl[0], 16)
    blk.sync(p3_sy)
    for i4 in range(4):
        d_ga[i4][1] += 64 * 16

    for cc in range(2):
        c0 = cc * 128

        def p3_pe(t, cc=cc, c0=c0):
            if cc == 0:
                t.wait_ge(s_ch[0], ch_cv)
            for cl in range(128):
                c = c0 + cl
                t.wait_ge(d_ga[c % 4][0], 16 * (c // 4 + 1))
                bank = pE if (cl // 8) % 2 == 0 else pF
                if cl % 8 == 0:
                    kbank = c // 8
                    if kbank >= 2:
                        t.wait_ge(m_cev[0], kbank - 1)
                s = cl % 8
                ins = None
                for dd in range(ND):
                    lhsT = AP(gbuf, (c % 4) * ND * 128 + dd * 128,
                              [[4 * ND * 128, 128], [1, 128]])
                    ncol = NM - dd
                    rhs = AP(KVR[cc], cl, [[L, 128], [128, ncol]])
                    outap = AP(bank, s * 64 + dd, [[512, 128], [1, ncol]])
                    ins = t.matmul(outap, lhsT, rhs, start=(dd == 0),
                                   stop=(dd == ND - 1))
                ins.then_inc(m_cv[0], 1)
        blk.tensor(p3_pe)
        m_cv[1] += 128

        def p3_act(sc, cc=cc, c0=c0):
            for kb_ in range(16):
                if cc == 1 and kb_ == 0:
                    sc.wait_ge(m_tr[0], T_y1[0])
                sc.wait_ge(m_cv[0], c0 + (kb_ + 1) * 8)
                bank = pE if kb_ % 2 == 0 else pF
                dst = AP(xA, kb_ * 8, [[L, 128], [1, 8], [128, NM]])
                src = AP(bank, 0, [[512, 128], [64, 8], [1, NM]])
                sc.copy(dst, src).then_inc(m_cev[0], 1)
        blk.scalar(p3_act)
        m_cev[1] += 16

        REVU = QU[cc]
        pipe_transpose(
            lambda g, j: xA[:, bass.ts(g * 4 + j, 128)],
            lambda g, REVU=REVU: AP(REVU, g * 512, [[PQK, 128], [1, 512]]),
            16, ch_cv, gate_sem=m_cev, gate_val=16 * (cc + 1))
        ch.join("vector", m_tev, m_tev[1])

        if "t_y1" in taps:
            ch.dma("scalar", AP(taps["t_y1"].tensor, cc * 128 * L,
                                [[L, 128], [1, L]]),
                   AP(REVU, 0, [[PQK, 128], [1, L]]))

        def g1(v, cc=cc, REVU=REVU):
            return v.scalar_tensor_tensor(
                AP(REVU, 0, [[PQK, 128], [1, L]]), VU[cc][:, :],
                pv[:, PV_DSK + cc:PV_DSK + cc + 1],
                AP(REVU, 0, [[PQK, 128], [1, L]]), ALU.mult, ALU.add)
        ch.step("vector", g1)

        def g2(v, cc=cc, REVU=REVU):
            return v.tensor_mul(AP(REVU, 0, [[PQK, 128], [1, L]]),
                                AP(REVU, 0, [[PQK, 128], [1, L]]),
                                AP(KU[cc], 0, [[PQK, 128], [1, L]]))
        ch.step("vector", g2)

        ch.dma("scalar", AP(y2cin, cc * 128 * L, [[L, 128], [1, L]]),
               AP(REVU, 0, [[PQK, 128], [1, L]]))

        csem = s_c2 if cc == 0 else s_c2b
        pend_y2 = list(ch.pend)

        def collh(gp, cc=cc, csem=csem, pend_y2=pend_y2, chv=s_ch[1]):
            gp.wait_ge(s_ch[0], chv)
            for sm, v in pend_y2:
                gp.wait_ge(sm[0], v)
            cin_h = AP(y2cin, cc * 128 * L, [[L, 128], [1, L]])
            gp.collective_compute(
                "AllGather", ALU.bypass, replica_groups=[list(range(NCORE))],
                ins=[cin_h.opt()], outs=[y2h[cc].ap().opt()]).then_inc(csem[0])
        blk.gpsimd(collh)

    es_mix.close()
    ln2t = sbE(es_ln2t, "ln2t", [128, NDC * TPC], BF16, side="right")
    woc = sbE(es_wo, "woc", [128, NDC * D], BF16, side="right")
    y2g = sbE(es_wo, "y2g", [128, NDC * TPC], BF16, side="right")
    borep = sbE(es_wo, "borep", [128, D], BF16, side="right")
    aoB = sbE(es_wo, "aoB", [128, NTC * D], BF16, side="right")

    dstw = AP(woc, 0, [[NDC * D, 128], [D, NDC], [1, D]])
    srcw = AP(woT.tensor, 0, [[D, 128], [128 * D, NDC], [1, D]])
    ch.dmas("gpsimd", [(dstw, srcw)])
    ch.join("gpsimd", s_c2, 1)
    ch.join("gpsimd", s_c2b, 1)
    if "t_y2" in taps:
        for cc_ in range(2):
            dstt = AP(taps["t_y2"].tensor, cc_ * 128 * L,
                      [[256 * L, NCORE], [1, 128 * L]])
            srct = AP(y2h[cc_], 0, [[128 * L, NCORE], [1, 128 * L]])
            ch.dma("scalar", dstt, srct)

    ch.snap()
    qsy = qsems["sync"]

    def y2g_load(sy):
        sy.wait_ge(s_ch[0], s_ch[1])
        for sm, v in ch.pend:
            sy.wait_ge(sm[0], v)
        pid = sy.partition_id()
        for hc in range(NDC):
            cc_, rank = hc % 2, hc // 2
            dst = AP(y2g, hc * TPC, [[NDC * TPC, 128], [1, TPC]])
            src = AP(y2h[cc_], rank * 128 * L + pid * TPC, [[L, 128], [1, TPC]])
            sy.dma_start(out=dst, in_=src).then_inc(qsy[0], 16)
    blk.sync(y2g_load)
    qsy[1] += 16 * NDC
    ch.pend = [(qsy, qsy[1])]

    for src_d, rep in ((bo_d, borep), (sdn, sdrep)):
        ch.dma("gpsimd", AP(scr, 0, [[D, 1], [1, D]]), src_d)

        def mm(t):
            ins = None
            for e in range(4):
                ins = t.matmul(PQ[e][:, :], ones1[:, :],
                               AP(scr, e * 512, [[D, 1], [1, 512]]),
                               start=True, stop=True)
            return ins
        ch.step("tensor", mm)

        def ev(v, rep=rep):
            ins = None
            for e in range(4):
                ins = v.tensor_copy(rep[:, bass.ts(e, 512)], PQ[e][:, :])
            return ins
        ch.step("vector", ev)

    ch_wo = ch.snap()

    def p5_pe(t):
        t.wait_ge(s_ch[0], ch_wo)
        for tc in range(NTC):
            if tc >= 1:
                t.wait_ge(m_aoev[0], tc)
            ins = None
            for hc in range(NDC):
                lhsT = AP(y2g, hc * TPC + tc * 128, [[NDC * TPC, 128], [1, 128]])
                for e in range(4):
                    ins = t.matmul(PQ[e][:, :], lhsT,
                                   AP(woc, hc * D + e * 512,
                                      [[NDC * D, 128], [1, 512]]),
                                   start=(hc == 0), stop=(hc == NDC - 1),
                                   skip_group_check=True)
            ins.then_inc(m_wo[0], 1)
    blk.tensor(p5_pe)
    m_wo[1] += NTC

    def p5_dve(v):
        for tc in range(NTC):
            v.wait_ge(m_wo[0], tc + 1)
            ins = None
            for e in range(4):
                ins = v.tensor_add(
                    AP(aoB, tc * D + e * 512, [[NTC * D, 128], [1, 512]]),
                    PQ[e][:, :], borep[:, bass.ts(e, 512)])
            ins.then_inc(m_aoev[0], 1)
    blk.vector(p5_dve)
    m_aoev[1] += NTC

    ch.join("vector", m_aoev, NTC)

    bn_batch(aoB, NTC * D, [tc * D for tc in range(NTC)])
    stats_finish(NTC)

    g5a = ch.snap()

    def p5a_sy(sy):
        sy.wait_ge(s_ch[0], g5a)
        for tc in range(NTC):
            sy.wait_ge(d_p5[0], 16 * tc)
            if tc >= 1:
                sy.wait_ge(m_5a[0], 7 * (tc - 1) + 2)
            sy.dma_start(out=x0t[:, :], in_=u_r[bass.ts(tc, 128), :]) \
                .then_inc(d_p5[0], 16)
    blk.sync(p5a_sy)
    d_p5[1] += NTC * 16

    def p5a_dv(v):
        v.wait_ge(s_ch[0], g5a)
        for tc in range(NTC):
            v.wait_ge(d_p5[0], 16 * (tc + 1))
            if tc >= 1:
                v.wait_ge(w_x1[0], 16 * tc)
                v.wait_ge(m_5a[0], 7 * tc)
            v.tensor_scalar(tmpf[:, :],
                            AP(aoB, tc * D, [[NTC * D, 128], [1, D]]),
                            stM[:, tc:tc + 1], stR[:, tc:tc + 1],
                            ALU.add, ALU.mult).then_inc(m_5a[0], 1)
            v.wait_ge(m_5a[0], 7 * tc + 1)
            v.tensor_add(tmpf[:, :], tmpf[:, :], x0t[:, :]) \
                .then_inc(m_5a[0], 1)
            for q4 in range(4):
                v.wait_ge(m_5a[0], 7 * tc + 2 + q4)
                v.bn_stats(stB[:, 24 * tc + 6 * q4:24 * tc + 6 * (q4 + 1)],
                           tmpf[:, bass.ts(q4, 512)]).then_inc(m_5a[0], 1)
            v.wait_ge(m_5a[0], 7 * tc + 6)
            v.bn_aggr(stA[:, 2 * tc:2 * tc + 2],
                      stB[:, 24 * tc:24 * (tc + 1)]).then_inc(m_5a[0], 1)
    blk.vector(p5a_dv)
    m_5a[1] += NTC * 7

    def p5a_sc(sc):
        for tc in range(NTC):
            sc.wait_ge(m_5a[0], 7 * tc + 2)
            sc.wait_ge(w_x1[0], 16 * tc)
            sc.dma_start(out=x1sp.ap()[bass.ts(tc, 128), :], in_=tmpf[:, :]) \
                .then_inc(w_x1[0], 16)
    blk.scalar(p5a_sc)
    w_x1[1] += NTC * 16

    ch.join("vector", m_5a, m_5a[1])
    ch.join("vector", w_x1, w_x1[1])
    stats_finish(NTC)
    if "t_x1" in taps:
        ch.dma("scalar", taps["t_x1"], x1sp.ap())

    g5b = ch.snap()
    T0 = m_tr[1]

    def p5b_sy(sy):
        sy.wait_ge(s_ch[0], g5b)
        for tc in range(NTC):
            sy.wait_ge(d_p5b[0], 16 * tc)
            if tc >= 1:
                sy.wait_ge(m_5n[0], tc)
            sy.dma_start(out=x0t[:, :], in_=x1sp.ap()[bass.ts(tc, 128), :]) \
                .then_inc(d_p5b[0], 16)
    blk.sync(p5b_sy)
    d_p5b[1] += NTC * 16

    def p5b_dv(v):
        v.wait_ge(s_ch[0], g5b)
        for tc in range(NTC):
            v.wait_ge(d_p5b[0], 16 * (tc + 1))
            if tc >= 1:
                v.wait_ge(m_tr[0], T0 + 4 * tc)
            v.tensor_scalar(scr[:, :], x0t[:, :], stM[:, tc:tc + 1],
                            stR[:, tc:tc + 1], ALU.add, ALU.mult) \
                .then_inc(m_5n[0], 1)
    blk.vector(p5b_dv)
    m_5n[1] += NTC

    def p5b_pe(t):
        t.wait_ge(s_ch[0], g5b)
        for g in range(NTC * 4):
            tc = g // 4
            n = T0 + g
            pt = PT[n % 2]
            t.wait_ge(m_5n[0], tc + 1)
            if n >= 2:
                t.wait_ge(m_tev[0], n - 1)
            ins = None
            for j in range(4):
                ins = t.transpose(pt[:, bass.ts(j, 128)],
                                  scr[:, bass.ts((g % 4) * 4 + j, 128)],
                                  ident[:, :])
            ins.then_inc(m_tr[0], 1)
    blk.tensor(p5b_pe)
    m_tr[1] += NTC * 4

    def p5b_act(sc):
        for g in range(NTC * 4):
            tc = g // 4
            n = T0 + g
            pt = PT[n % 2]
            sc.wait_ge(m_tr[0], n + 1)
            dst = AP(ln2t, ((g % 4) * 4) * TPC + tc * 128,
                     [[NDC * TPC, 128], [TPC, 4], [1, 128]])
            sc.copy(dst, AP(pt, 0, [[1024, 128], [128, 4], [1, 128]])) \
                .then_inc(m_tev[0], 1)
    blk.scalar(p5b_act)
    m_tev[1] += NTC * 4
    trn[0] += NTC * 4
    ch.join("vector", m_tev, m_tev[1])

    es_p5.close()
    es_wo.close()
    pG = es.enter_context(nc.psum_tensor("pG", [128, 512], F32, side="right"))
    pH = es.enter_context(nc.psum_tensor("pH", [128, 512], F32, side="right"))
    hid = sbE(es, "hid", [128, NHC * TPC], BF16)
    gsil = sbE(es, "gsil", [128, TPC], BF16)
    wgs = sbE(es, "wgs", [128, 2 * NDC * 128], BF16)
    wus = sbE(es, "wus", [128, 2 * NDC * 128], BF16)
    PFX = [[pA, pB, pC, pD], [pE, pF, pG, pH]]

    ch_f = ch.snap()
    p6_gate = ch_f

    def p6_gp(gp):
        gp.wait_ge(s_ch[0], ch_f)
        for hc in range(NHC):
            if hc >= 2:
                gp.wait_ge(m_fpe[0], hc - 1)
            for wsl, wtd, dsem in ((wgs, wgT, d_fg), (wus, wuT, d_fu)):
                sl = dsem[hc % 2]
                gp.wait_ge(sl[0], 16 * (hc // 2))
                dst = AP(wsl, (hc % 2) * NDC * 128,
                         [[2 * NDC * 128, 128], [128, NDC], [1, 128]])
                src = AP(wtd.tensor, 128 * hc,
                         [[HID, 128], [128 * HID, NDC], [1, 128]])
                gp.dma_start(out=dst, in_=src).then_inc(sl[0], 16)
    blk.gpsimd(p6_gp)
    for dsem in (d_fg, d_fu):
        dsem[0][1] += (NHC + 1) // 2 * 16
        dsem[1][1] += NHC // 2 * 16

    def p6_pe(t):
        t.wait_ge(s_ch[0], ch_f)
        for hc in range(NHC):
            t.wait_ge(d_fg[hc % 2][0], 16 * (hc // 2 + 1))
            t.wait_ge(d_fu[hc % 2][0], 16 * (hc // 2 + 1))
            bk = PFX[hc % 2]
            if hc >= 2:
                t.wait_ge(m_fsi[0], hc - 1)
                t.wait_ge(m_fhd[0], hc - 1)
            ins = None
            for dc in range(NDC):
                lg = AP(wgs, (hc % 2) * NDC * 128 + dc * 128,
                        [[2 * NDC * 128, 128], [1, 128]])
                lu = AP(wus, (hc % 2) * NDC * 128 + dc * 128,
                        [[2 * NDC * 128, 128], [1, 128]])
                for half in range(2):
                    rhs = AP(ln2t, dc * TPC + half * 512,
                             [[NDC * TPC, 128], [1, 512]])
                    ins = t.matmul(bk[half][:, :], lg, rhs, start=(dc == 0),
                                   stop=(dc == NDC - 1), skip_group_check=True)
                for half in range(2):
                    rhs = AP(ln2t, dc * TPC + half * 512,
                             [[NDC * TPC, 128], [1, 512]])
                    ins = t.matmul(bk[2 + half][:, :], lu, rhs, start=(dc == 0),
                                   stop=(dc == NDC - 1), skip_group_check=True)
            ins.then_inc(m_fpe[0], 1)
    blk.tensor(p6_pe)
    m_fpe[1] += NHC

    def p6_act(sc):
        for hc in range(NHC):
            sc.wait_ge(m_fpe[0], hc + 1)
            if hc >= 1:
                sc.wait_ge(m_fhd[0], hc)
            bk = PFX[hc % 2]
            ins = None
            for half in range(2):
                ins = sc.activation(gsil[:, bass.ts(half, 512)], bk[half][:, :],
                                    AF.Silu,
                                    scale=pv[:, PV_SG + hc:PV_SG + hc + 1])
            ins.then_inc(m_fsi[0], 1)
    blk.scalar(p6_act)
    m_fsi[1] += NHC

    def p6_dve(v):
        for hc in range(NHC):
            v.wait_ge(m_fsi[0], hc + 1)
            bk = PFX[hc % 2]
            ins = None
            for half in range(2):
                ins = v.scalar_tensor_tensor(
                    hid[:, hc * TPC + half * 512:hc * TPC + (half + 1) * 512],
                    bk[2 + half][:, :], pv[:, PV_SU + hc:PV_SU + hc + 1],
                    gsil[:, bass.ts(half, 512)], ALU.mult, ALU.mult)
            ins.then_inc(m_fhd[0], 1)
    blk.vector(p6_dve)
    m_fhd[1] += NHC

    ch.join("scalar", m_fhd, NHC)
    if "t_hid" in taps:
        ch.dma("scalar", taps["t_hid"], AP(hid, 0, [[NHC * TPC, 128],
                                                    [1, NHC * TPC]]))

    es_ln2t.close()
    dwB = sbE(es, "dwB", [128, NTC * D], BF16, side="right")
    x0u = sbE(es, "x0u", [128, D], F32)
    PALL = [pA, pB, pC, pD, pE, pF, pG, pH]

    ch_d = ch.snap()
    p7_gate = p6_gate

    def p7_sy(sy):
        sy.wait_ge(s_ch[0], p7_gate)
        for slw in w_wc:
            if slw[1] > 0:
                sy.wait_ge(slw[0], slw[1])
        for gs in range(4 * NHC):
            hc = gs % NHC
            sl = d_wd[gs % 3]
            sy.wait_ge(sl[0], 16 * (gs // 3))
            if gs >= 3:
                sy.wait_ge(m_dn[0], gs - 2)
            dst = AP(wds4, (gs % 3) * D, [[3 * D, 128], [1, D]])
            src = AP(wdbf, hc * 128 * D, [[D, 128], [1, D]])
            sy.dma_start(out=dst, in_=src).then_inc(sl[0], 16)
    blk.sync(p7_sy)
    for i4 in range(3):
        d_wd[i4][1] += len(range(i4, 4 * NHC, 3)) * 16

    def p7_pe(t):
        t.wait_ge(s_ch[0], ch_d)
        for tsup in range(4):
            tca, tcb = 2 * tsup, 2 * tsup + 1
            if tsup >= 1:
                t.wait_ge(m_dwev[0], tsup)
            for hc in range(NHC):
                gs = tsup * NHC + hc
                t.wait_ge(d_wd[gs % 3][0], 16 * (gs // 3 + 1))
                ins = None
                for half, tcx in ((0, tca), (1, tcb)):
                    lhsT = AP(hid, hc * TPC + tcx * 128,
                              [[NHC * TPC, 128], [1, 128]])
                    for e in range(4):
                        rhs = AP(wds4, (gs % 3) * D + e * 512,
                                 [[3 * D, 128], [1, 512]])
                        ins = t.matmul(PALL[half * 4 + e][:, :], lhsT, rhs,
                                       start=(hc == 0), stop=(hc == NHC - 1),
                                       skip_group_check=True)
                ins.then_inc(m_dn[0], 1)
    blk.tensor(p7_pe)
    m_dn[1] += 4 * NHC

    def p7_dve(v):
        for tsup in range(4):
            v.wait_ge(m_dn[0], (tsup + 1) * NHC)
            ins = None
            for half, tcx in ((0, 2 * tsup), (1, 2 * tsup + 1)):
                for e in range(4):
                    ins = v.tensor_mul(
                        AP(dwB, tcx * D + e * 512, [[NTC * D, 128], [1, 512]]),
                        PALL[half * 4 + e][:, :], sdrep[:, bass.ts(e, 512)])
            ins.then_inc(m_dwev[0], 1)
    blk.vector(p7_dve)
    m_dwev[1] += 4

    def p8_bn(v):
        k = 0
        for tsup in range(4):
            v.wait_ge(m_dwev[0], m_dwev[1] - 4 + tsup + 1)
            for tc in (2 * tsup, 2 * tsup + 1):
                for q4 in range(4):
                    if k > 0:
                        v.wait_ge(m_8s[0], k)
                    v.bn_stats(
                        stB[:, 24 * tc + 6 * q4:24 * tc + 6 * (q4 + 1)],
                        AP(dwB, tc * D + q4 * 512, [[NTC * D, 128], [1, 512]])) \
                        .then_inc(m_8s[0], 1)
                    k += 1
                v.wait_ge(m_8s[0], k)
                v.bn_aggr(stA[:, 2 * tc:2 * tc + 2],
                          stB[:, 24 * tc:24 * (tc + 1)]).then_inc(m_8s[0], 1)
                k += 1
    blk.vector(p8_bn)
    m_8s[1] += 40

    ch.join("vector", m_dwev, 4)
    ch.join("vector", m_8s, m_8s[1])

    stats_finish(NTC)
    g8 = ch.snap()

    def p8_sy(sy):
        sy.wait_ge(s_ch[0], g8)
        for tc in range(NTC):
            sy.wait_ge(d_p8[0], 16 * tc)
            if tc >= 2:
                sy.wait_ge(m_8[0], 2 * (tc - 1))
            slot = x0t if tc % 2 == 0 else x0u
            sy.dma_start(out=slot[:, :], in_=x1sp.ap()[bass.ts(tc, 128), :]) \
                .then_inc(d_p8[0], 16)
    blk.sync(p8_sy)
    d_p8[1] += NTC * 16

    def p8_dv(v):
        v.wait_ge(s_ch[0], g8)
        for tc in range(NTC):
            v.wait_ge(d_p8[0], 16 * (tc + 1))
            if tc >= 1:
                v.wait_ge(w_out[0], 16 * tc)
                v.wait_ge(m_8[0], 2 * tc)
            v.tensor_scalar(tmpf[:, :],
                            AP(dwB, tc * D, [[NTC * D, 128], [1, D]]),
                            stM[:, tc:tc + 1], stR[:, tc:tc + 1],
                            ALU.add, ALU.mult).then_inc(m_8[0], 1)
            v.wait_ge(m_8[0], 2 * tc + 1)
            slot = x0t if tc % 2 == 0 else x0u
            v.tensor_add(tmpf[:, :], tmpf[:, :], slot[:, :]) \
                .then_inc(m_8[0], 1)
    blk.vector(p8_dv)
    m_8[1] += NTC * 2

    def p8_sc(sc):
        for tc in range(NTC):
            sc.wait_ge(m_8[0], 2 * tc + 2)
            sc.wait_ge(w_out[0], 16 * tc)
            sc.dma_start(out=out_r[bass.ts(tc, 128), :], in_=tmpf[:, :]) \
                .then_inc(w_out[0], 16)
    blk.scalar(p8_sc)
    w_out[1] += NTC * 16

    def fin_wait(sy):
        sy.wait_ge(s_ch[0], s_ch[1])
        for qq in qsems.values():
            if qq[1] > 0:
                sy.wait_ge(qq[0], qq[1])
        for sl in d_wd:
            sy.wait_ge(sl[0], sl[1])
        sy.wait_ge(w_out[0], w_out[1])
    blk.sync(fin_wait)

    es.close()
    return nc, taps


_CACHE = {}


def _prep(inputs):
    u = np.asarray(inputs["L0_attn_u"]).reshape(L, D)
    wq = np.ascontiguousarray(np.asarray(inputs["L0_attn_Wq"]).T)
    wk = np.ascontiguousarray(np.asarray(inputs["L0_attn_Wk"]).T)
    wv = np.ascontiguousarray(np.asarray(inputs["L0_attn_Wv"]).T)
    wo = np.ascontiguousarray(np.asarray(inputs["L0_attn_Wo"]).T)
    bq = np.asarray(inputs["L0_attn_bq"])
    bk = np.asarray(inputs["L0_attn_bk"])
    bv = np.asarray(inputs["L0_attn_bv"])
    bo = np.ascontiguousarray(np.asarray(inputs["L0_attn_bo"]).reshape(1, D))
    cqw = np.asarray(inputs["L0_attn_convq_w"]).reshape(D, 3)
    cqb = np.asarray(inputs["L0_attn_convq_b"])
    ckw = np.asarray(inputs["L0_attn_convk_w"]).reshape(D, 3)
    ckb = np.asarray(inputs["L0_attn_convk_b"])
    ssm = np.asarray(inputs["L0_attn_ssm_kernel"])
    dsk = np.asarray(inputs["L0_attn_D_skip"])
    wg = np.ascontiguousarray(np.asarray(inputs["L0_ffn_wq_gate"]).T)
    wu = np.ascontiguousarray(np.asarray(inputs["L0_ffn_wq_up"]).T)
    wd = np.ascontiguousarray(np.asarray(inputs["L0_ffn_wq_down"]).T)
    sg = np.asarray(inputs["L0_ffn_s_gate"])
    su = np.asarray(inputs["L0_ffn_s_up"])
    sd = np.ascontiguousarray(np.asarray(inputs["L0_ffn_s_down"]).reshape(1, D))

    in_maps = []
    for c in range(NCORE):
        s0 = c * CPC
        pvarr = np.zeros((128, NPV), np.float32)
        for cc in range(2):
            sl = slice(s0 + 128 * cc, s0 + 128 * (cc + 1))
            pvarr[:, PV_BQ + cc] = bq[sl]
            pvarr[:, PV_BK + cc] = bk[sl]
            pvarr[:, PV_BV + cc] = bv[sl]
            pvarr[:, PV_CQB + cc] = cqb[sl]
            pvarr[:, PV_CKB + cc] = ckb[sl]
            pvarr[:, PV_DSK + cc] = dsk[sl]
            for j in range(3):
                pvarr[:, PV_CQW + 2 * j + cc] = cqw[sl, j]
                pvarr[:, PV_CKW + 2 * j + cc] = ckw[sl, j]
        for hc in range(NHC):
            pvarr[:, PV_SG + hc] = sg[128 * hc:128 * (hc + 1)]
            pvarr[:, PV_SU + hc] = su[128 * hc:128 * (hc + 1)]
        in_maps.append({
            "u_r": np.ascontiguousarray(u[c * TPC:(c + 1) * TPC, :]),
            "wqT": np.ascontiguousarray(wq[:, s0:s0 + CPC]),
            "wkT": np.ascontiguousarray(wk[:, s0:s0 + CPC]),
            "wvT": np.ascontiguousarray(wv[:, s0:s0 + CPC]),
            "pvec": pvarr,
            "ssm": np.ascontiguousarray(ssm[s0:s0 + CPC, :]),
            "woT": wo,
            "bo_d": bo,
            "sdn": sd,
            "wgT": wg,
            "wuT": wu,
            "wdT": wd,
        })
    return in_maps


def kernel(**inputs):
    in_maps = _prep(inputs)
    if "nc" not in _CACHE:
        _CACHE["nc"] = build_nc()[0]
    nc = _CACHE["nc"]
    res = run_bass_kernel_spmd(nc, in_maps, core_ids=list(range(NCORE)))
    out = np.concatenate([res.results[c]["out_r"] for c in range(NCORE)], axis=0)
    return out.reshape(1, L, D).astype(np.float32)


# version 11
# speedup vs baseline: 1.0036x; 1.0036x over previous
import contextlib
import numpy as np
import concourse.bass as bass
import concourse.mybir as mybir
from concourse.bass_utils import run_bass_kernel_spmd

F32 = mybir.dt.float32
BF16 = mybir.dt.bfloat16
I32 = mybir.dt.int32
AF = mybir.ActivationFunctionType
ALU = mybir.AluOpType
AX = mybir.AxisListType

NCORE = 8
D = 2048
L = 8192
HID = 5632
TPC = L // NCORE
CPC = D // NCORE
NTC = TPC // 128
NDC = D // 128
NHC = HID // 128
NM = L // 128
ND = 17
KP = 2304
PADL = 8
EPS = 1e-5

PV_BQ, PV_BK, PV_BV, PV_CQB, PV_CKB, PV_DSK = 0, 2, 4, 6, 8, 10
PV_CQW, PV_CKW, PV_SG, PV_SU = 12, 18, 24, 68
NPV = 112


class Ch:

    def __init__(self, blk, sem, qsems):
        self.blk = blk
        self.s = sem
        self.q = qsems
        self.pend = []

    def step(self, eng, emit, k=1):
        s = self.s
        pend = list(self.pend)

        def _(e):
            e.wait_ge(s[0], s[1] - k)
            for sm, v in pend:
                e.wait_ge(sm[0], v)
            emit(e).then_inc(s[0], k)
        s[1] += k
        self.pend = []
        getattr(self.blk, eng)(_)

    def dma(self, eng, out, in_):
        self.dmas(eng, [(out, in_)])

    def dmas(self, eng, pairs):
        s = self.s
        qs = self.q[eng]
        pend = list(self.pend)

        def _(e):
            e.wait_ge(s[0], s[1])
            for sm, v in pend:
                e.wait_ge(sm[0], v)
            for out, in_ in pairs:
                e.dma_start(out=out, in_=in_).then_inc(qs[0], 16)
        qs[1] += 16 * len(pairs)
        self.pend = [(qs, qs[1])]
        getattr(self.blk, eng)(_)

    def join(self, eng, sem, val):
        s = self.s
        pend = list(self.pend)

        def _(e):
            e.wait_ge(sem[0], val)
            e.wait_ge(s[0], s[1] - 1)
            for sm, v in pend:
                e.wait_ge(sm[0], v)
            (e.engine_nop() if hasattr(e, "engine_nop") else e.nop()) \
                .then_inc(s[0], 1)
        s[1] += 1
        self.pend = []
        getattr(self.blk, eng)(_)

    def snap(self):
        if self.pend:
            self.step("vector", lambda v: v.engine_nop())
        return self.s[1]


def build_nc(tap=()):
    nc = bass.Bass("TRN2", target_bir_lowering=False, debug=False)

    u_r = nc.dram_tensor("u_r", [TPC, D], F32, kind="ExternalInput").ap()
    wqT = nc.dram_tensor("wqT", [D, CPC], F32, kind="ExternalInput").ap()
    wkT = nc.dram_tensor("wkT", [D, CPC], F32, kind="ExternalInput").ap()
    wvT = nc.dram_tensor("wvT", [D, CPC], F32, kind="ExternalInput").ap()
    pvec = nc.dram_tensor("pvec", [128, NPV], F32, kind="ExternalInput").ap()
    ssm = nc.dram_tensor("ssm", [CPC, 2048], F32, kind="ExternalInput").ap()
    woT = nc.dram_tensor("woT", [D, D], F32, kind="ExternalInput").ap()
    bo_d = nc.dram_tensor("bo_d", [1, D], F32, kind="ExternalInput").ap()
    sdn = nc.dram_tensor("sdn", [1, D], F32, kind="ExternalInput").ap()
    wgT = nc.dram_tensor("wgT", [D, HID], I32, kind="ExternalInput").ap()
    wuT = nc.dram_tensor("wuT", [D, HID], I32, kind="ExternalInput").ap()
    wdT = nc.dram_tensor("wdT", [HID, D], I32, kind="ExternalInput").ap()
    out_r = nc.dram_tensor("out_r", [TPC, D], F32, kind="ExternalOutput").ap()

    ln1cin = nc.dram_tensor("ln1cin", [D, TPC], BF16)
    ln1all = nc.dram_tensor("ln1all", [NCORE * D, TPC], BF16,
                            addr_space="Shared")
    y2cin = nc.dram_tensor("y2cin", [CPC, L], BF16)
    y2h = [nc.dram_tensor(f"y2h{i}", [NCORE * 128, L], BF16,
                          addr_space="Shared") for i in range(2)]
    kpbuf = nc.dram_tensor("kpbuf", [CPC, KP], BF16)
    x1sp = nc.dram_tensor("x1sp", [TPC, D], F32)
    wdbf = nc.dram_tensor("wdbf", [HID, D], BF16)

    taps = {}
    for name, shape, dt in tap:
        taps[name] = nc.dram_tensor(name, shape, dt, kind="ExternalOutput").ap()

    es = contextlib.ExitStack()
    es_ln = contextlib.ExitStack()
    es_mix = contextlib.ExitStack()
    es_qkv = contextlib.ExitStack()
    es_ln2t = contextlib.ExitStack()
    es_wo = contextlib.ExitStack()
    es_p5 = contextlib.ExitStack()

    def sbE(stack, name, shape, dtype, side="left"):
        return stack.enter_context(nc.sbuf_tensor(name, shape, dtype, side=side))

    pv = sbE(es, "pv", [128, NPV], F32)
    ident = sbE(es, "ident", [128, 128], BF16)
    ones1 = sbE(es, "ones1", [1, 128], BF16)
    cst = sbE(es, "cst", [128, 2], F32)
    x0t = sbE(es, "x0t", [128, D], F32)
    scr = sbE(es, "scr", [128, D], BF16)
    tmpf = sbE(es, "tmpf", [128, D], F32)
    stB = sbE(es, "stB", [128, 24 * NTC], F32)
    stA = sbE(es, "stA", [128, 2 * NTC], F32)
    stM = sbE(es, "stM", [128, NTC], F32)
    stR = sbE(es, "stR", [128, NTC], F32)

    qk_shape = [128, PADL + L]
    q0 = sbE(es_mix, "q0", qk_shape, BF16, side="right")
    q1 = sbE(es_mix, "q1", qk_shape, BF16, side="right")
    k0 = sbE(es_mix, "k0", qk_shape, BF16, side="right")
    k1 = sbE(es_mix, "k1", qk_shape, BF16, side="right")
    v0 = sbE(es_mix, "v0", [128, L], BF16, side="right")
    v1 = sbE(es_mix, "v1", [128, L], BF16, side="right")
    QU, KU, VU = (q0, q1), (k0, k1), (v0, v1)

    x0B = sbE(es_ln, "x0B", [128, NTC * D], BF16, side="right")
    lnTB = sbE(es_ln, "lnTB", [128, NDC * TPC], BF16, side="right")

    pA = es.enter_context(nc.psum_tensor("pA", [128, 512], F32))
    pB = es.enter_context(nc.psum_tensor("pB", [128, 512], F32))
    pC = es.enter_context(nc.psum_tensor("pC", [128, 512], F32))
    pD = es.enter_context(nc.psum_tensor("pD", [128, 512], F32))
    pE = es.enter_context(nc.psum_tensor("pE", [128, 512], F32))
    pF = es.enter_context(nc.psum_tensor("pF", [128, 512], F32))
    pT1 = es_p5.enter_context(nc.psum_tensor("pT1", [128, 1024], BF16,
                                             side="right"))
    pT2 = es_p5.enter_context(nc.psum_tensor("pT2", [128, 1024], BF16,
                                             side="right"))
    PQ = [pA, pB, pC, pD]
    PT = [pT1, pT2]

    def sem(name):
        return [es.enter_context(nc.semaphore(name)), 0]

    s_ch = sem("s_ch")
    d_rhs = [sem("d_rhs0"), sem("d_rhs1")]
    m_qkv, m_qev = sem("m_qkv"), sem("m_qev")
    m_tr, m_tev = sem("m_tr"), sem("m_tev")
    d_ga = [sem("d_ga0"), sem("d_ga1"), sem("d_ga2"), sem("d_ga3")]
    d_gb = [sem("d_gb0"), sem("d_gb1"), sem("d_gb2"), sem("d_gb3")]
    m_cv, m_cev = sem("m_cv"), sem("m_cev")
    s_c1, s_c2 = sem("s_c1"), sem("s_c2")
    s_c1b, s_c2b, s_c2c, s_c2d = sem("s_c1b"), sem("s_c2b"), sem("s_c2c"), sem("s_c2d")
    d_wc = [sem("d_wc0"), sem("d_wc1"), sem("d_wc2"), sem("d_wc3")]
    w_wc = [sem("w_wc0"), sem("w_wc1"), sem("w_wc2"), sem("w_wc3")]
    m_wc = sem("m_wc")
    m_wo, m_aoev = sem("m_wo"), sem("m_aoev")
    d_fg = [sem("d_fg0"), sem("d_fg1")]
    d_fu = [sem("d_fu0"), sem("d_fu1")]
    m_fpe, m_fsi, m_fhd = sem("m_fpe"), sem("m_fsi"), sem("m_fhd")
    d_wd = [sem("d_wd0"), sem("d_wd1"), sem("d_wd2"), sem("d_wd3")]
    m_dn, m_dwev = sem("m_dn"), sem("m_dwev")
    d_p5, w_x1, m_5a = sem("d_p5"), sem("w_x1"), sem("m_5a")
    d_p5b, m_5n = sem("d_p5b"), sem("m_5n")
    d_p8, m_8, w_out = sem("d_p8"), sem("m_8"), sem("w_out")
    m_8s, m_5s = sem("m_8s"), sem("m_5s")

    def AP(t, off, dims):
        return bass.AP(t, off, dims)

    nc.const_aps.register(F32, float(EPS), cst[:, 0:1])
    nc.const_aps.register(F32, -0.5, cst[:, 1:2])

    qsems = {"sync": sem("q_sy"), "gpsimd": sem("q_gp"),
             "scalar": sem("q_sc"), "vector": sem("q_dv")}
    blk = es.enter_context(nc.Block())
    ch = Ch(blk, s_ch, qsems)

    def bn_batch(src_sb, src_pitch, offs):
        for j, o in enumerate(offs):
            for q4 in range(4):
                def st(v, j=j, o=o, q4=q4):
                    return v.bn_stats(
                        stB[:, 24 * j + 6 * q4:24 * j + 6 * (q4 + 1)],
                        AP(src_sb, o + q4 * 512, [[src_pitch, 128], [1, 512]]))
                ch.step("vector", st)
        for j in range(len(offs)):
            def ag(v, j=j):
                return v.bn_aggr(stA[:, 2 * j:2 * j + 2],
                                 stB[:, 24 * j:24 * (j + 1)])
            ch.step("vector", ag)

    def stats_finish(n):
        mview = AP(stA, 0, [[2 * NTC, 128], [2, n]])
        vview = AP(stA, 1, [[2 * NTC, 128], [2, n]])

        def neg(v):
            return v.tensor_scalar(stM[:, 0:n], mview, -1.0, None, ALU.mult)
        ch.step("vector", neg)

        def ln_(sc):
            return sc.activation(stR[:, 0:n], vview, AF.Ln, bias=float(EPS))
        ch.step("scalar", ln_)

        def ex_(sc):
            return sc.activation(stR[:, 0:n], stR[:, 0:n], AF.Exp, scale=-0.5)
        ch.step("scalar", ex_)

    def p0a_ident(gp):
        return gp.memset(ident[:, :], 0.0)
    ch.step("gpsimd", p0a_ident, 1)

    def p0a_sets(gp):
        gp.affine_select(out=ident[:, :], in_=ident[:, :],
                         compare_op=ALU.not_equal, fill=1.0, base=0,
                         pattern=[[-1, 128]], channel_multiplier=1)
        gp.memset(ones1[:, :], 1.0)
        gp.memset(cst[:, 0:1], float(EPS))
        gp.memset(cst[:, 1:2], -0.5)
        ins = None
        for u in (q0, q1, k0, k1):
            ins = gp.memset(u[:, 0:PADL], 0.0)
        return ins
    ch.step("gpsimd", p0a_sets, 1)

    x0B_dst = AP(x0B, 0, [[NTC * D, 128], [D, NTC], [1, D]])
    x0B_src = AP(u_r.tensor, 0, [[D, 128], [128 * D, NTC], [1, D]])
    ch.dmas("gpsimd", [(pv[:, :], pvec), (x0B_dst, x0B_src)])

    bn_batch(x0B, NTC * D, [tc * D for tc in range(NTC)])
    stats_finish(NTC)
    for tc in range(NTC):
        def nrm(v, tc=tc):
            sl = AP(x0B, tc * D, [[NTC * D, 128], [1, D]])
            return v.tensor_scalar(sl, sl, stM[:, tc:tc + 1],
                                   stR[:, tc:tc + 1], ALU.add, ALU.mult)
        ch.step("vector", nrm)

    trn = [0]

    def pipe_transpose(src_fn, dst_fn, ngroups, ch_gate, gate_sem=None,
                       gate_val=0):
        t0 = trn[0]

        def pe(t, t0=t0, ch_gate=ch_gate):
            t.wait_ge(s_ch[0], ch_gate)
            if gate_sem is not None:
                t.wait_ge(gate_sem[0], gate_val)
            for g in range(ngroups):
                n = t0 + g
                pt = PT[n % 2]
                if n >= 2:
                    t.wait_ge(m_tev[0], n - 1)
                ins = None
                for j in range(4):
                    ins = t.transpose(pt[:, bass.ts(j, 128)], src_fn(g, j),
                                      ident[:, :])
                ins.then_inc(m_tr[0], 1)
        blk.tensor(pe)
        m_tr[1] += ngroups

        def act(sc, t0=t0):
            for g in range(ngroups):
                n = t0 + g
                pt = PT[n % 2]
                sc.wait_ge(m_tr[0], n + 1)
                sc.copy(dst_fn(g), AP(pt, 0, [[1024, 128], [128, 4], [1, 128]])) \
                    .then_inc(m_tev[0], 1)
        blk.scalar(act)
        m_tev[1] += ngroups
        trn[0] += ngroups

    gate0 = ch.snap()
    tev0 = m_tev[1]
    pipe_transpose(
        lambda g, j: AP(x0B, (g // 4) * D + ((g % 4) * 4 + j) * 128,
                        [[NTC * D, 128], [1, 128]]),
        lambda g: AP(lnTB, ((g % 4) * 4) * TPC + (g // 4) * 128,
                     [[NDC * TPC, 128], [TPC, 4], [1, 128]]),
        NTC * 4, gate0)
    ch.join("vector", m_tev, tev0 + 32)
    ch.dma("scalar",
           AP(ln1cin, 0, [[TPC, 128], [128 * TPC, NDC], [1, TPC]]),
           AP(lnTB, 0, [[NDC * TPC, 128], [TPC, NDC], [1, TPC]]))
    pend1 = list(ch.pend)

    def coll1(gp, pend1=pend1, chv=s_ch[1]):
        gp.wait_ge(s_ch[0], chv)
        for sm, v in pend1:
            gp.wait_ge(sm[0], v)
        gp.collective_compute(
            "AllGather", ALU.bypass, replica_groups=[list(range(NCORE))],
            ins=[ln1cin.ap().opt()], outs=[ln1all.ap().opt()]) \
            .then_inc(s_c1[0])
    blk.gpsimd(coll1)

    es_ln.close()
    wds4 = sbE(es, "wds4", [128, 4 * D], BF16)
    wq_sb = sbE(es_qkv, "wq_sb", [128, NDC * CPC], BF16, side="right")
    wk_sb = sbE(es_qkv, "wk_sb", [128, NDC * CPC], BF16, side="right")
    wv_sb = sbE(es_qkv, "wv_sb", [128, NDC * CPC], BF16, side="right")
    kb = sbE(es_qkv, "kb", [128, 2 * KP], BF16, side="right")
    rhsb = sbE(es_qkv, "rhsb", [128, 2 * NDC * 512], BF16, side="right")

    def p0c_kb(gp):
        return gp.memset(kb[:, :], 0.0)
    ch.step("gpsimd", p0c_kb, 1)
    loadpairs = []
    for wsb, wdr in ((wq_sb, wqT), (wk_sb, wkT), (wv_sb, wvT)):
        dst = AP(wsb, 0, [[NDC * CPC, 128], [CPC, NDC], [1, CPC]])
        src = AP(wdr.tensor, 0, [[CPC, 128], [128 * CPC, NDC], [1, CPC]])
        loadpairs.append((dst, src))
    for cc in range(2):
        dst = AP(kb, cc * KP + 127, [[2 * KP, 128], [1, 2048]])
        src = AP(ssm.tensor, cc * 128 * 2048, [[2048, 128], [1, 2048]])
        loadpairs.append((dst, src))
    ch.dmas("gpsimd", loadpairs)
    ch.dmas("scalar", [
        (AP(kpbuf, cc * 128 * KP, [[KP, 128], [1, KP]]),
         AP(kb, cc * KP, [[2 * KP, 128], [1, KP]])) for cc in range(2)])
    ch.join("gpsimd", s_c1, 1)
    if "t_ln1" in taps:
        ch.dma("scalar", taps["t_ln1"], ln1all.ap())

    ch_qkv = ch.snap()
    WSB = (wq_sb, wk_sb, wv_sb)
    UNITS = (QU, KU, VU)
    BIASC = (PV_BQ, PV_BK, PV_BV)

    ch_wc = ch_qkv

    def wc_gp(gp):
        gp.wait_ge(s_ch[0], ch_wc)
        for hc in range(NHC + 2):
            if hc < NHC:
                sl = d_wc[hc % 4]
                gp.wait_ge(sl[0], 16 * (hc // 4))
                if hc >= 4:
                    gp.wait_ge(w_wc[hc % 4][0], 16 * (hc // 4))
                dst = AP(wds4, (hc % 4) * D, [[4 * D, 128], [1, D]])
                src = AP(wdT.tensor, hc * 128 * D, [[D, 128], [1, D]])
                gp.dma_start(out=dst, in_=src).then_inc(sl[0], 16)
            wb = hc - 2
            if 0 <= wb < NHC:
                gp.wait_ge(m_wc[0], wb + 1)
                slw = w_wc[wb % 4]
                gp.wait_ge(slw[0], 16 * (wb // 4))
                wdst = AP(wdbf, wb * 128 * D, [[D, 128], [1, D]])
                wsrc = AP(wds4, (wb % 4) * D, [[4 * D, 128], [1, D]])
                gp.dma_start(out=wdst, in_=wsrc).then_inc(slw[0], 16)
    blk.gpsimd(wc_gp)
    for i4 in range(4):
        d_wc[i4][1] += len(range(i4, NHC, 4)) * 16
        w_wc[i4][1] += len(range(i4, NHC, 4)) * 16

    def wc_dw(v):
        v.wait_ge(s_ch[0], ch_wc)
        for hc in range(NHC):
            v.wait_ge(d_wc[hc % 4][0], 16 * (hc // 4 + 1))
            sl = AP(wds4, (hc % 4) * D, [[4 * D, 128], [1, D]])
            v.tensor_scalar_add(sl, sl, -128.0).then_inc(m_wc[0], 1)
    blk.vector(wc_dw)
    m_wc[1] += NHC

    def p1_sync(sy):
        sy.wait_ge(s_ch[0], ch_qkv)
        sy.wait_ge(s_c1[0], 1)
        for i in range(16):
            sl = d_rhs[i % 2]
            sy.wait_ge(sl[0], 16 * (i // 2))
            if i >= 2:
                sy.wait_ge(m_qkv[0], (i - 1) * 6)
            r, h = i // 2, i % 2
            dst = AP(rhsb, (i % 2) * NDC * 512,
                     [[2 * NDC * 512, 128], [512, NDC], [1, 512]])
            src = AP(ln1all, r * D * TPC + h * 512,
                     [[TPC, 128], [128 * TPC, NDC], [1, 512]])
            sy.dma_start(out=dst, in_=src).then_inc(sl[0], 16)
    blk.sync(p1_sync)
    d_rhs[0][1] += 16 * 8
    d_rhs[1][1] += 16 * 8

    def p1_pe(t):
        for i in range(16):
            t.wait_ge(d_rhs[i % 2][0], 16 * (i // 2 + 1))
            for jj in range(6):
                n = 6 * i + jj
                proj, cc = jj // 2, jj % 2
                bank = PQ[n % 4]
                if n >= 4:
                    t.wait_ge(m_qev[0], n - 3)
                ins = None
                for dc in range(NDC):
                    lhsT = AP(WSB[proj], dc * CPC + cc * 128,
                              [[NDC * CPC, 128], [1, 128]])
                    rhs = AP(rhsb, (i % 2) * NDC * 512 + dc * 512,
                             [[2 * NDC * 512, 128], [1, 512]])
                    ins = t.matmul(bank[:, :], lhsT, rhs,
                                   start=(dc == 0), stop=(dc == NDC - 1))
                ins.then_inc(m_qkv[0], 1)
    blk.tensor(p1_pe)
    m_qkv[1] += 96

    def p1_act(sc):
        for n in range(96):
            sc.wait_ge(m_qkv[0], n + 1)
            i, jj = n // 6, n % 6
            proj, cc = jj // 2, jj % 2
            unit = UNITS[proj][cc]
            pitch = (PADL + L) if proj < 2 else L
            off = (PADL if proj < 2 else 0) + i * 512
            dst = AP(unit, off, [[pitch, 128], [1, 512]])
            sc.add(dst, PQ[n % 4][:, :],
                   pv[:, BIASC[proj] + cc:BIASC[proj] + cc + 1]) \
                .then_inc(m_qev[0], 1)
    blk.scalar(p1_act)
    m_qev[1] += 96

    ch.join("vector", m_qev, 96)

    PQK = PADL + L

    def emit_sconv(src_unit, dst_sb, dst_pitch, wcol, bcol):
        def c1(v):
            return v.tensor_scalar(
                AP(dst_sb, 0, [[dst_pitch, 128], [1, L]]),
                AP(src_unit, PADL, [[PQK, 128], [1, L]]),
                pv[:, wcol + 4:wcol + 5], pv[:, bcol:bcol + 1],
                ALU.mult, ALU.add)
        ch.step("vector", c1)

        def c2(v):
            return v.scalar_tensor_tensor(
                AP(dst_sb, 0, [[dst_pitch, 128], [1, L]]),
                AP(src_unit, PADL - 1, [[PQK, 128], [1, L]]),
                pv[:, wcol + 2:wcol + 3],
                AP(dst_sb, 0, [[dst_pitch, 128], [1, L]]), ALU.mult, ALU.add)
        ch.step("vector", c2)

        def c3(v):
            return v.scalar_tensor_tensor(
                AP(dst_sb, 0, [[dst_pitch, 128], [1, L]]),
                AP(src_unit, PADL - 2, [[PQK, 128], [1, L]]),
                pv[:, wcol + 0:wcol + 1],
                AP(dst_sb, 0, [[dst_pitch, 128], [1, L]]), ALU.mult, ALU.add)
        ch.step("vector", c3)

    for cc in range(2):
        emit_sconv(KU[cc], rhsb, 2 * NDC * 512, PV_CKW + cc, PV_CKB + cc)

        def mkv(v, cc=cc):
            return v.tensor_mul(VU[cc][:, :],
                                AP(rhsb, 0, [[2 * NDC * 512, 128], [1, L]]),
                                VU[cc][:, :])
        ch.step("vector", mkv)
    for cc in range(2):
        emit_sconv(QU[cc], KU[cc], PQK, PV_CQW + cc, PV_CQB + cc)

    if "t_kv" in taps:
        for cc in range(2):
            ch.dma("scalar", AP(taps["t_kv"].tensor, cc * 128 * L,
                                [[L, 128], [1, L]]), VU[cc][:, :])
    if "t_qt" in taps:
        for cc in range(2):
            ch.dma("scalar", AP(taps["t_qt"].tensor, cc * 128 * L,
                                [[L, 128], [1, L]]),
                   AP(KU[cc], 0, [[PQK, 128], [1, L]]))

    es_qkv.close()
    kvr = sbE(es_mix, "kvr", [128, L], BF16, side="right")
    kvr2 = sbE(es_mix, "kvr2", [128, L], BF16, side="right")
    xA = sbE(es_mix, "xA", [128, L], BF16, side="right")
    gbuf = sbE(es_mix, "gbuf", [128, 4 * ND * 128], BF16, side="right")

    KVR = (kvr, kvr2)
    for cc in range(2):
        REVU = QU[cc]

        def rev(v, cc=cc, REVU=REVU):
            dst = AP(REVU, 0, [[PQK, 128], [128, NM], [1, 128]])
            src = AP(VU[cc], 127, [[L, 128], [128, NM], [-1, 128]])
            return v.tensor_copy(dst, src)
        ch.step("vector", rev)

        ch_here = ch.snap()
        pipe_transpose(
            lambda g, j, REVU=REVU: AP(REVU, (g * 4 + j) * 128,
                                       [[PQK, 128], [1, 128]]),
            lambda g, cc=cc: KVR[cc][:, bass.ts(g, 512)],
            16, ch_here)
    ch.join("vector", m_tev, m_tev[1])

    ch_cv = ch.snap()
    T_y1 = [trn[0] + 16, trn[0] + 32]

    def p3_sy(sy):
        sy.wait_ge(s_ch[0], ch_cv)
        for c in range(256):
            sl = d_ga[c % 4]
            sy.wait_ge(sl[0], 16 * (c // 4))
            if c >= 4:
                sy.wait_ge(m_cv[0], c - 3)
            dst = AP(gbuf, (c % 4) * ND * 128,
                     [[4 * ND * 128, 128], [1, ND * 128]])
            src = AP(kpbuf, c * KP, [[1, 128], [1, ND * 128]])
            sy.dma_start(out=dst, in_=src).then_inc(sl[0], 16)
    blk.sync(p3_sy)
    for i4 in range(4):
        d_ga[i4][1] += 64 * 16

    for cc in range(2):
        c0 = cc * 128

        def p3_pe(t, cc=cc, c0=c0):
            if cc == 0:
                t.wait_ge(s_ch[0], ch_cv)
            for cl in range(128):
                c = c0 + cl
                t.wait_ge(d_ga[c % 4][0], 16 * (c // 4 + 1))
                bank = pE if (cl // 8) % 2 == 0 else pF
                if cl % 8 == 0:
                    kbank = c // 8
                    if kbank >= 2:
                        t.wait_ge(m_cev[0], kbank - 1)
                s = cl % 8
                ins = None
                for dd in range(ND):
                    lhsT = AP(gbuf, (c % 4) * ND * 128 + dd * 128,
                              [[4 * ND * 128, 128], [1, 128]])
                    ncol = NM - dd
                    rhs = AP(KVR[cc], cl, [[L, 128], [128, ncol]])
                    outap = AP(bank, s * 64 + dd, [[512, 128], [1, ncol]])
                    ins = t.matmul(outap, lhsT, rhs, start=(dd == 0),
                                   stop=(dd == ND - 1))
                ins.then_inc(m_cv[0], 1)
        blk.tensor(p3_pe)
        m_cv[1] += 128

        def p3_act(sc, cc=cc, c0=c0):
            for kb_ in range(16):
                if cc == 1 and kb_ == 0:
                    sc.wait_ge(m_tr[0], T_y1[0])
                sc.wait_ge(m_cv[0], c0 + (kb_ + 1) * 8)
                bank = pE if kb_ % 2 == 0 else pF
                dst = AP(xA, kb_ * 8, [[L, 128], [1, 8], [128, NM]])
                src = AP(bank, 0, [[512, 128], [64, 8], [1, NM]])
                sc.copy(dst, src).then_inc(m_cev[0], 1)
        blk.scalar(p3_act)
        m_cev[1] += 16

        REVU = QU[cc]
        pipe_transpose(
            lambda g, j: xA[:, bass.ts(g * 4 + j, 128)],
            lambda g, REVU=REVU: AP(REVU, g * 512, [[PQK, 128], [1, 512]]),
            16, ch_cv, gate_sem=m_cev, gate_val=16 * (cc + 1))
        ch.join("vector", m_tev, m_tev[1])

        if "t_y1" in taps:
            ch.dma("scalar", AP(taps["t_y1"].tensor, cc * 128 * L,
                                [[L, 128], [1, L]]),
                   AP(REVU, 0, [[PQK, 128], [1, L]]))

        def g1(v, cc=cc, REVU=REVU):
            return v.scalar_tensor_tensor(
                AP(REVU, 0, [[PQK, 128], [1, L]]), VU[cc][:, :],
                pv[:, PV_DSK + cc:PV_DSK + cc + 1],
                AP(REVU, 0, [[PQK, 128], [1, L]]), ALU.mult, ALU.add)
        ch.step("vector", g1)

        def g2(v, cc=cc, REVU=REVU):
            return v.tensor_mul(AP(REVU, 0, [[PQK, 128], [1, L]]),
                                AP(REVU, 0, [[PQK, 128], [1, L]]),
                                AP(KU[cc], 0, [[PQK, 128], [1, L]]))
        ch.step("vector", g2)

        ch.dma("scalar", AP(y2cin, cc * 128 * L, [[L, 128], [1, L]]),
               AP(REVU, 0, [[PQK, 128], [1, L]]))

        csem = s_c2 if cc == 0 else s_c2b
        pend_y2 = list(ch.pend)

        def collh(gp, cc=cc, csem=csem, pend_y2=pend_y2, chv=s_ch[1]):
            gp.wait_ge(s_ch[0], chv)
            for sm, v in pend_y2:
                gp.wait_ge(sm[0], v)
            cin_h = AP(y2cin, cc * 128 * L, [[L, 128], [1, L]])
            gp.collective_compute(
                "AllGather", ALU.bypass, replica_groups=[list(range(NCORE))],
                ins=[cin_h.opt()], outs=[y2h[cc].ap().opt()]).then_inc(csem[0])
        blk.gpsimd(collh)

    es_mix.close()
    ln2t = sbE(es_ln2t, "ln2t", [128, NDC * TPC], BF16, side="right")
    woc = sbE(es_wo, "woc", [128, NDC * D], BF16, side="right")
    y2g = sbE(es_wo, "y2g", [128, NDC * TPC], BF16, side="right")
    borep = sbE(es_wo, "borep", [128, D], BF16, side="right")
    aoB = sbE(es_wo, "aoB", [128, NTC * D], BF16, side="right")

    dstw = AP(woc, 0, [[NDC * D, 128], [D, NDC], [1, D]])
    srcw = AP(woT.tensor, 0, [[D, 128], [128 * D, NDC], [1, D]])
    ch.dmas("gpsimd", [(dstw, srcw)])
    ch.join("gpsimd", s_c2, 1)
    ch.join("gpsimd", s_c2b, 1)
    if "t_y2" in taps:
        for cc_ in range(2):
            dstt = AP(taps["t_y2"].tensor, cc_ * 128 * L,
                      [[256 * L, NCORE], [1, 128 * L]])
            srct = AP(y2h[cc_], 0, [[128 * L, NCORE], [1, 128 * L]])
            ch.dma("scalar", dstt, srct)

    ch.snap()
    qsy = qsems["sync"]

    def y2g_load(sy):
        sy.wait_ge(s_ch[0], s_ch[1])
        for sm, v in ch.pend:
            sy.wait_ge(sm[0], v)
        pid = sy.partition_id()
        for hc in range(NDC):
            cc_, rank = hc % 2, hc // 2
            dst = AP(y2g, hc * TPC, [[NDC * TPC, 128], [1, TPC]])
            src = AP(y2h[cc_], rank * 128 * L + pid * TPC, [[L, 128], [1, TPC]])
            sy.dma_start(out=dst, in_=src).then_inc(qsy[0], 16)
    blk.sync(y2g_load)
    qsy[1] += 16 * NDC
    ch.pend = [(qsy, qsy[1])]

    for src_d, rep in ((bo_d, borep),):
        ch.dma("gpsimd", AP(scr, 0, [[D, 1], [1, D]]), src_d)

        def mm(t):
            ins = None
            for e in range(4):
                ins = t.matmul(PQ[e][:, :], ones1[:, :],
                               AP(scr, e * 512, [[D, 1], [1, 512]]),
                               start=True, stop=True)
            return ins
        ch.step("tensor", mm)

        def ev(v, rep=rep):
            ins = None
            for e in range(4):
                ins = v.tensor_copy(rep[:, bass.ts(e, 512)], PQ[e][:, :])
            return ins
        ch.step("vector", ev)

    ch_wo = ch.snap()

    def p5_pe(t):
        t.wait_ge(s_ch[0], ch_wo)
        for tc in range(NTC):
            if tc >= 1:
                t.wait_ge(m_aoev[0], tc)
            ins = None
            for hc in range(NDC):
                lhsT = AP(y2g, hc * TPC + tc * 128, [[NDC * TPC, 128], [1, 128]])
                for e in range(4):
                    ins = t.matmul(PQ[e][:, :], lhsT,
                                   AP(woc, hc * D + e * 512,
                                      [[NDC * D, 128], [1, 512]]),
                                   start=(hc == 0), stop=(hc == NDC - 1),
                                   skip_group_check=True)
            ins.then_inc(m_wo[0], 1)
    blk.tensor(p5_pe)
    m_wo[1] += NTC

    def p5_dve(v):
        for tc in range(NTC):
            v.wait_ge(m_wo[0], tc + 1)
            ins = None
            for e in range(4):
                ins = v.tensor_add(
                    AP(aoB, tc * D + e * 512, [[NTC * D, 128], [1, 512]]),
                    PQ[e][:, :], borep[:, bass.ts(e, 512)])
            ins.then_inc(m_aoev[0], 1)
    blk.vector(p5_dve)
    m_aoev[1] += NTC

    def p5_bn(v):
        k = 0
        for tc in range(NTC):
            v.wait_ge(m_aoev[0], tc + 1)
            for q4 in range(4):
                if k > 0:
                    v.wait_ge(m_5s[0], k)
                v.bn_stats(stB[:, 24 * tc + 6 * q4:24 * tc + 6 * (q4 + 1)],
                           AP(aoB, tc * D + q4 * 512, [[NTC * D, 128], [1, 512]])) \
                    .then_inc(m_5s[0], 1)
                k += 1
            v.wait_ge(m_5s[0], k)
            v.bn_aggr(stA[:, 2 * tc:2 * tc + 2],
                      stB[:, 24 * tc:24 * (tc + 1)]).then_inc(m_5s[0], 1)
            k += 1
    blk.vector(p5_bn)
    m_5s[1] += NTC * 5

    ch.join("vector", m_aoev, NTC)
    ch.join("vector", m_5s, m_5s[1])
    stats_finish(NTC)

    g5a = ch.snap()

    def p5a_sy(sy):
        sy.wait_ge(s_ch[0], g5a)
        for tc in range(NTC):
            sy.wait_ge(d_p5[0], 16 * tc)
            if tc >= 1:
                sy.wait_ge(m_5a[0], 7 * (tc - 1) + 2)
            sy.dma_start(out=x0t[:, :], in_=u_r[bass.ts(tc, 128), :]) \
                .then_inc(d_p5[0], 16)
    blk.sync(p5a_sy)
    d_p5[1] += NTC * 16

    def p5a_dv(v):
        v.wait_ge(s_ch[0], g5a)
        for tc in range(NTC):
            v.wait_ge(d_p5[0], 16 * (tc + 1))
            if tc >= 1:
                v.wait_ge(w_x1[0], 16 * tc)
                v.wait_ge(m_5a[0], 7 * tc)
            v.tensor_scalar(tmpf[:, :],
                            AP(aoB, tc * D, [[NTC * D, 128], [1, D]]),
                            stM[:, tc:tc + 1], stR[:, tc:tc + 1],
                            ALU.add, ALU.mult).then_inc(m_5a[0], 1)
            v.wait_ge(m_5a[0], 7 * tc + 1)
            v.tensor_add(tmpf[:, :], tmpf[:, :], x0t[:, :]) \
                .then_inc(m_5a[0], 1)
            for q4 in range(4):
                v.wait_ge(m_5a[0], 7 * tc + 2 + q4)
                v.bn_stats(stB[:, 24 * tc + 6 * q4:24 * tc + 6 * (q4 + 1)],
                           tmpf[:, bass.ts(q4, 512)]).then_inc(m_5a[0], 1)
            v.wait_ge(m_5a[0], 7 * tc + 6)
            v.bn_aggr(stA[:, 2 * tc:2 * tc + 2],
                      stB[:, 24 * tc:24 * (tc + 1)]).then_inc(m_5a[0], 1)
    blk.vector(p5a_dv)
    m_5a[1] += NTC * 7

    def p5a_sc(sc):
        for tc in range(NTC):
            sc.wait_ge(m_5a[0], 7 * tc + 2)
            sc.wait_ge(w_x1[0], 16 * tc)
            sc.dma_start(out=x1sp.ap()[bass.ts(tc, 128), :], in_=tmpf[:, :]) \
                .then_inc(w_x1[0], 16)
    blk.scalar(p5a_sc)
    w_x1[1] += NTC * 16

    ch.join("vector", m_5a, m_5a[1])
    ch.join("vector", w_x1, w_x1[1])
    stats_finish(NTC)
    if "t_x1" in taps:
        ch.dma("scalar", taps["t_x1"], x1sp.ap())

    g5b = ch.snap()
    T0 = m_tr[1]

    def p5b_sy(sy):
        sy.wait_ge(s_ch[0], g5b)
        for tc in range(NTC):
            sy.wait_ge(d_p5b[0], 16 * tc)
            if tc >= 1:
                sy.wait_ge(m_5n[0], tc)
            sy.dma_start(out=x0t[:, :], in_=x1sp.ap()[bass.ts(tc, 128), :]) \
                .then_inc(d_p5b[0], 16)
    blk.sync(p5b_sy)
    d_p5b[1] += NTC * 16

    def p5b_dv(v):
        v.wait_ge(s_ch[0], g5b)
        for tc in range(NTC):
            v.wait_ge(d_p5b[0], 16 * (tc + 1))
            if tc >= 1:
                v.wait_ge(m_tr[0], T0 + 4 * tc)
            v.tensor_scalar(scr[:, :], x0t[:, :], stM[:, tc:tc + 1],
                            stR[:, tc:tc + 1], ALU.add, ALU.mult) \
                .then_inc(m_5n[0], 1)
    blk.vector(p5b_dv)
    m_5n[1] += NTC

    def p5b_pe(t):
        t.wait_ge(s_ch[0], g5b)
        for g in range(NTC * 4):
            tc = g // 4
            n = T0 + g
            pt = PT[n % 2]
            t.wait_ge(m_5n[0], tc + 1)
            if n >= 2:
                t.wait_ge(m_tev[0], n - 1)
            ins = None
            for j in range(4):
                ins = t.transpose(pt[:, bass.ts(j, 128)],
                                  scr[:, bass.ts((g % 4) * 4 + j, 128)],
                                  ident[:, :])
            ins.then_inc(m_tr[0], 1)
    blk.tensor(p5b_pe)
    m_tr[1] += NTC * 4

    def p5b_act(sc):
        for g in range(NTC * 4):
            tc = g // 4
            n = T0 + g
            pt = PT[n % 2]
            sc.wait_ge(m_tr[0], n + 1)
            dst = AP(ln2t, ((g % 4) * 4) * TPC + tc * 128,
                     [[NDC * TPC, 128], [TPC, 4], [1, 128]])
            sc.copy(dst, AP(pt, 0, [[1024, 128], [128, 4], [1, 128]])) \
                .then_inc(m_tev[0], 1)
    blk.scalar(p5b_act)
    m_tev[1] += NTC * 4
    trn[0] += NTC * 4
    ch.join("vector", m_tev, m_tev[1])

    es_p5.close()
    es_wo.close()
    pG = es.enter_context(nc.psum_tensor("pG", [128, 512], F32, side="right"))
    pH = es.enter_context(nc.psum_tensor("pH", [128, 512], F32, side="right"))
    hid = sbE(es, "hid", [128, NHC * TPC], BF16)
    gsil = sbE(es, "gsil", [128, TPC], BF16)
    wgs = sbE(es, "wgs", [128, 2 * NDC * 128], BF16)
    wus = sbE(es, "wus", [128, 2 * NDC * 128], BF16)
    PFX = [[pA, pB, pC, pD], [pE, pF, pG, pH]]

    ch_f = ch.snap()
    p6_gate = ch_f

    def p6_gp(gp):
        gp.wait_ge(s_ch[0], ch_f)
        for hc in range(NHC):
            if hc >= 2:
                gp.wait_ge(m_fpe[0], hc - 1)
            for wsl, wtd, dsem in ((wgs, wgT, d_fg), (wus, wuT, d_fu)):
                sl = dsem[hc % 2]
                gp.wait_ge(sl[0], 16 * (hc // 2))
                dst = AP(wsl, (hc % 2) * NDC * 128,
                         [[2 * NDC * 128, 128], [128, NDC], [1, 128]])
                src = AP(wtd.tensor, 128 * hc,
                         [[HID, 128], [128 * HID, NDC], [1, 128]])
                gp.dma_start(out=dst, in_=src).then_inc(sl[0], 16)
    blk.gpsimd(p6_gp)
    for dsem in (d_fg, d_fu):
        dsem[0][1] += (NHC + 1) // 2 * 16
        dsem[1][1] += NHC // 2 * 16

    def p6_pe(t):
        t.wait_ge(s_ch[0], ch_f)
        for hc in range(NHC):
            t.wait_ge(d_fg[hc % 2][0], 16 * (hc // 2 + 1))
            t.wait_ge(d_fu[hc % 2][0], 16 * (hc // 2 + 1))
            bk = PFX[hc % 2]
            if hc >= 2:
                t.wait_ge(m_fsi[0], hc - 1)
                t.wait_ge(m_fhd[0], hc - 1)
            ins = None
            for dc in range(NDC):
                lg = AP(wgs, (hc % 2) * NDC * 128 + dc * 128,
                        [[2 * NDC * 128, 128], [1, 128]])
                lu = AP(wus, (hc % 2) * NDC * 128 + dc * 128,
                        [[2 * NDC * 128, 128], [1, 128]])
                for half in range(2):
                    rhs = AP(ln2t, dc * TPC + half * 512,
                             [[NDC * TPC, 128], [1, 512]])
                    ins = t.matmul(bk[half][:, :], lg, rhs, start=(dc == 0),
                                   stop=(dc == NDC - 1), skip_group_check=True)
                for half in range(2):
                    rhs = AP(ln2t, dc * TPC + half * 512,
                             [[NDC * TPC, 128], [1, 512]])
                    ins = t.matmul(bk[2 + half][:, :], lu, rhs, start=(dc == 0),
                                   stop=(dc == NDC - 1), skip_group_check=True)
            ins.then_inc(m_fpe[0], 1)
    blk.tensor(p6_pe)
    m_fpe[1] += NHC

    def p6_act(sc):
        for hc in range(NHC):
            sc.wait_ge(m_fpe[0], hc + 1)
            if hc >= 1:
                sc.wait_ge(m_fhd[0], hc)
            bk = PFX[hc % 2]
            ins = None
            for half in range(2):
                ins = sc.activation(gsil[:, bass.ts(half, 512)], bk[half][:, :],
                                    AF.Silu,
                                    scale=pv[:, PV_SG + hc:PV_SG + hc + 1])
            ins.then_inc(m_fsi[0], 1)
    blk.scalar(p6_act)
    m_fsi[1] += NHC

    def p6_dve(v):
        for hc in range(NHC):
            v.wait_ge(m_fsi[0], hc + 1)
            bk = PFX[hc % 2]
            ins = None
            for half in range(2):
                ins = v.scalar_tensor_tensor(
                    hid[:, hc * TPC + half * 512:hc * TPC + (half + 1) * 512],
                    bk[2 + half][:, :], pv[:, PV_SU + hc:PV_SU + hc + 1],
                    gsil[:, bass.ts(half, 512)], ALU.mult, ALU.mult)
            ins.then_inc(m_fhd[0], 1)
    blk.vector(p6_dve)
    m_fhd[1] += NHC

    ch.join("scalar", m_fhd, NHC)
    if "t_hid" in taps:
        ch.dma("scalar", taps["t_hid"], AP(hid, 0, [[NHC * TPC, 128],
                                                    [1, NHC * TPC]]))

    es_ln2t.close()
    dwB = sbE(es, "dwB", [128, NTC * D], BF16, side="right")
    x0u = sbE(es, "x0u", [128, D], F32)
    sdrep = sbE(es, "sdrep", [128, D], F32)
    ch.dma("gpsimd", AP(scr, 0, [[D, 1], [1, D]]), sdn)

    def sdmm(t):
        ins = None
        for e in range(4):
            ins = t.matmul(PQ[e][:, :], ones1[:, :],
                           AP(scr, e * 512, [[D, 1], [1, 512]]),
                           start=True, stop=True)
        return ins
    ch.step("tensor", sdmm)

    def sdev(v):
        ins = None
        for e in range(4):
            ins = v.tensor_copy(sdrep[:, bass.ts(e, 512)], PQ[e][:, :])
        return ins
    ch.step("vector", sdev)
    PALL = [pA, pB, pC, pD, pE, pF, pG, pH]

    ch_d = ch.snap()
    p7_gate = p6_gate

    def p7_sy(sy):
        sy.wait_ge(s_ch[0], p7_gate)
        for slw in w_wc:
            if slw[1] > 0:
                sy.wait_ge(slw[0], slw[1])
        for gs in range(4 * NHC):
            hc = gs % NHC
            sl = d_wd[gs % 4]
            sy.wait_ge(sl[0], 16 * (gs // 4))
            if gs >= 4:
                sy.wait_ge(m_dn[0], gs - 3)
            dst = AP(wds4, (gs % 4) * D, [[4 * D, 128], [1, D]])
            src = AP(wdbf, hc * 128 * D, [[D, 128], [1, D]])
            sy.dma_start(out=dst, in_=src).then_inc(sl[0], 16)
    blk.sync(p7_sy)
    for i4 in range(4):
        d_wd[i4][1] += len(range(i4, 4 * NHC, 4)) * 16

    def p7_pe(t):
        t.wait_ge(s_ch[0], ch_d)
        for tsup in range(4):
            tca, tcb = 2 * tsup, 2 * tsup + 1
            if tsup >= 1:
                t.wait_ge(m_dwev[0], tsup)
            for hc in range(NHC):
                gs = tsup * NHC + hc
                t.wait_ge(d_wd[gs % 4][0], 16 * (gs // 4 + 1))
                ins = None
                for half, tcx in ((0, tca), (1, tcb)):
                    lhsT = AP(hid, hc * TPC + tcx * 128,
                              [[NHC * TPC, 128], [1, 128]])
                    for e in range(4):
                        rhs = AP(wds4, (gs % 4) * D + e * 512,
                                 [[4 * D, 128], [1, 512]])
                        ins = t.matmul(PALL[half * 4 + e][:, :], lhsT, rhs,
                                       start=(hc == 0), stop=(hc == NHC - 1),
                                       skip_group_check=True)
                ins.then_inc(m_dn[0], 1)
    blk.tensor(p7_pe)
    m_dn[1] += 4 * NHC

    def p7_dve(v):
        for tsup in range(4):
            v.wait_ge(m_dn[0], (tsup + 1) * NHC)
            ins = None
            for half, tcx in ((0, 2 * tsup), (1, 2 * tsup + 1)):
                for e in range(4):
                    ins = v.tensor_mul(
                        AP(dwB, tcx * D + e * 512, [[NTC * D, 128], [1, 512]]),
                        PALL[half * 4 + e][:, :], sdrep[:, bass.ts(e, 512)])
            ins.then_inc(m_dwev[0], 1)
    blk.vector(p7_dve)
    m_dwev[1] += 4

    def p8_bn(v):
        k = 0
        for tsup in range(4):
            v.wait_ge(m_dwev[0], m_dwev[1] - 4 + tsup + 1)
            for tc in (2 * tsup, 2 * tsup + 1):
                for q4 in range(4):
                    if k > 0:
                        v.wait_ge(m_8s[0], k)
                    v.bn_stats(
                        stB[:, 24 * tc + 6 * q4:24 * tc + 6 * (q4 + 1)],
                        AP(dwB, tc * D + q4 * 512, [[NTC * D, 128], [1, 512]])) \
                        .then_inc(m_8s[0], 1)
                    k += 1
                v.wait_ge(m_8s[0], k)
                v.bn_aggr(stA[:, 2 * tc:2 * tc + 2],
                          stB[:, 24 * tc:24 * (tc + 1)]).then_inc(m_8s[0], 1)
                k += 1
    blk.vector(p8_bn)
    m_8s[1] += 40

    ch.join("vector", m_dwev, 4)
    ch.join("vector", m_8s, m_8s[1])

    stats_finish(NTC)
    g8 = ch.snap()

    def p8_sy(sy):
        sy.wait_ge(s_ch[0], g8)
        for tc in range(NTC):
            sy.wait_ge(d_p8[0], 16 * tc)
            if tc >= 2:
                sy.wait_ge(m_8[0], 2 * (tc - 1))
            slot = x0t if tc % 2 == 0 else x0u
            sy.dma_start(out=slot[:, :], in_=x1sp.ap()[bass.ts(tc, 128), :]) \
                .then_inc(d_p8[0], 16)
    blk.sync(p8_sy)
    d_p8[1] += NTC * 16

    def p8_dv(v):
        v.wait_ge(s_ch[0], g8)
        for tc in range(NTC):
            v.wait_ge(d_p8[0], 16 * (tc + 1))
            if tc >= 1:
                v.wait_ge(w_out[0], 16 * tc)
                v.wait_ge(m_8[0], 2 * tc)
            v.tensor_scalar(tmpf[:, :],
                            AP(dwB, tc * D, [[NTC * D, 128], [1, D]]),
                            stM[:, tc:tc + 1], stR[:, tc:tc + 1],
                            ALU.add, ALU.mult).then_inc(m_8[0], 1)
            v.wait_ge(m_8[0], 2 * tc + 1)
            slot = x0t if tc % 2 == 0 else x0u
            v.tensor_add(tmpf[:, :], tmpf[:, :], slot[:, :]) \
                .then_inc(m_8[0], 1)
    blk.vector(p8_dv)
    m_8[1] += NTC * 2

    def p8_sc(sc):
        for tc in range(NTC):
            sc.wait_ge(m_8[0], 2 * tc + 2)
            sc.wait_ge(w_out[0], 16 * tc)
            sc.dma_start(out=out_r[bass.ts(tc, 128), :], in_=tmpf[:, :]) \
                .then_inc(w_out[0], 16)
    blk.scalar(p8_sc)
    w_out[1] += NTC * 16

    def fin_wait(sy):
        sy.wait_ge(s_ch[0], s_ch[1])
        for qq in qsems.values():
            if qq[1] > 0:
                sy.wait_ge(qq[0], qq[1])
        for sl in d_wd:
            sy.wait_ge(sl[0], sl[1])
        sy.wait_ge(w_out[0], w_out[1])
    blk.sync(fin_wait)

    es.close()
    return nc, taps


_CACHE = {}


def _prep(inputs):
    u = np.asarray(inputs["L0_attn_u"]).reshape(L, D)
    wq = np.ascontiguousarray(np.asarray(inputs["L0_attn_Wq"]).T)
    wk = np.ascontiguousarray(np.asarray(inputs["L0_attn_Wk"]).T)
    wv = np.ascontiguousarray(np.asarray(inputs["L0_attn_Wv"]).T)
    wo = np.ascontiguousarray(np.asarray(inputs["L0_attn_Wo"]).T)
    bq = np.asarray(inputs["L0_attn_bq"])
    bk = np.asarray(inputs["L0_attn_bk"])
    bv = np.asarray(inputs["L0_attn_bv"])
    bo = np.ascontiguousarray(np.asarray(inputs["L0_attn_bo"]).reshape(1, D))
    cqw = np.asarray(inputs["L0_attn_convq_w"]).reshape(D, 3)
    cqb = np.asarray(inputs["L0_attn_convq_b"])
    ckw = np.asarray(inputs["L0_attn_convk_w"]).reshape(D, 3)
    ckb = np.asarray(inputs["L0_attn_convk_b"])
    ssm = np.asarray(inputs["L0_attn_ssm_kernel"])
    dsk = np.asarray(inputs["L0_attn_D_skip"])
    wg = np.ascontiguousarray(np.asarray(inputs["L0_ffn_wq_gate"]).T)
    wu = np.ascontiguousarray(np.asarray(inputs["L0_ffn_wq_up"]).T)
    wd = np.ascontiguousarray(np.asarray(inputs["L0_ffn_wq_down"]).T)
    sg = np.asarray(inputs["L0_ffn_s_gate"])
    su = np.asarray(inputs["L0_ffn_s_up"])
    sd = np.ascontiguousarray(np.asarray(inputs["L0_ffn_s_down"]).reshape(1, D))

    in_maps = []
    for c in range(NCORE):
        s0 = c * CPC
        pvarr = np.zeros((128, NPV), np.float32)
        for cc in range(2):
            sl = slice(s0 + 128 * cc, s0 + 128 * (cc + 1))
            pvarr[:, PV_BQ + cc] = bq[sl]
            pvarr[:, PV_BK + cc] = bk[sl]
            pvarr[:, PV_BV + cc] = bv[sl]
            pvarr[:, PV_CQB + cc] = cqb[sl]
            pvarr[:, PV_CKB + cc] = ckb[sl]
            pvarr[:, PV_DSK + cc] = dsk[sl]
            for j in range(3):
                pvarr[:, PV_CQW + 2 * j + cc] = cqw[sl, j]
                pvarr[:, PV_CKW + 2 * j + cc] = ckw[sl, j]
        for hc in range(NHC):
            pvarr[:, PV_SG + hc] = sg[128 * hc:128 * (hc + 1)]
            pvarr[:, PV_SU + hc] = su[128 * hc:128 * (hc + 1)]
        in_maps.append({
            "u_r": np.ascontiguousarray(u[c * TPC:(c + 1) * TPC, :]),
            "wqT": np.ascontiguousarray(wq[:, s0:s0 + CPC]),
            "wkT": np.ascontiguousarray(wk[:, s0:s0 + CPC]),
            "wvT": np.ascontiguousarray(wv[:, s0:s0 + CPC]),
            "pvec": pvarr,
            "ssm": np.ascontiguousarray(ssm[s0:s0 + CPC, :]),
            "woT": wo,
            "bo_d": bo,
            "sdn": sd,
            "wgT": wg,
            "wuT": wu,
            "wdT": wd,
        })
    return in_maps


def kernel(**inputs):
    in_maps = _prep(inputs)
    if "nc" not in _CACHE:
        _CACHE["nc"] = build_nc()[0]
    nc = _CACHE["nc"]
    res = run_bass_kernel_spmd(nc, in_maps, core_ids=list(range(NCORE)))
    out = np.concatenate([res.results[c]["out_r"] for c in range(NCORE)], axis=0)
    return out.reshape(1, L, D).astype(np.float32)
